# version 21
# baseline (speedup 1.0000x reference)
import math
import numpy as np
import contextlib
import concourse.bass as bass
import concourse.mybir as mybir
from concourse.bass_utils import run_bass_kernel_spmd
from concourse.alu_op_type import AluOpType as OP

F32 = mybir.dt.float32
BF16 = mybir.dt.bfloat16
AF = mybir.ActivationFunctionType
AX = mybir.AxisListType

NEG_BIG = -1.0e30
STRIPE_CHUNKS = 3


class Cfg:
    def __init__(self, B=2, S=4096, D=2048, H=16, I=8192, n_cores=8):
        self.B, self.S, self.D, self.H, self.I, self.n_cores = B, S, D, H, I, n_cores
        assert D // H == 128 and D % 128 == 0
        assert H % n_cores == 0
        self.HPC = H // n_cores
        self.PW = self.HPC * 128
        self.NDM = D // 128
        self.NT = S // 128
        assert (B * S) % n_cores == 0
        self.TOK = B * S // n_cores
        assert self.TOK % 128 == 0
        self.THALF = min(512, self.TOK)
        assert self.TOK % self.THALF == 0
        self.NHALF = self.TOK // self.THALF
        self.NTT = self.THALF // 128
        self.ICH = 256
        assert I % self.ICH == 0
        self.NIC = I // self.ICH
        self.ECH = min(512, D)
        self.NE = D // self.ECH


ENGINES = ("sync", "tensor", "vector", "scalar", "gpsimd")


class Sem:
    __slots__ = ("h", "v")

    def __init__(self, handle):
        self.h = handle
        self.v = 0


class Rec:

    SELF_SERIAL = ("vector", "scalar", "gpsimd")

    def __init__(self, selfsems):
        self.ops = {e: [] for e in ENGINES}
        self.selfsems = selfsems
        self.last_token = {}

    def emit(self, eng, fn):
        self.ops[eng].append(fn)

    def wait(self, eng, sem, val):
        if val <= 0:
            return
        self.emit(eng, lambda e, s=sem.h, v=val: e.wait_ge(s, v))

    def op(self, eng, method, *args, inc=None, inc_n=1, **kwargs):
        if eng in self.SELF_SERIAL:
            tok = self.last_token.get(eng)
            if tok is not None:
                self.wait(eng, tok[0], tok[1])
            if inc is None:
                inc, inc_n = self.selfsems[eng], 1
        if inc is not None:
            inc.v += inc_n
            if eng in self.SELF_SERIAL:
                self.last_token[eng] = (inc, inc.v)

        def fn(e, m=method, a=args, k=kwargs, s=(None if inc is None else inc.h), n=inc_n):
            inst = getattr(e, m)(*a, **k)
            if s is not None:
                inst.then_inc(s, n)
        self.emit(eng, fn)

    def play(self, nc):
        with nc.Block() as blk:
            for name in ENGINES:
                lst = self.ops[name]
                if not lst:
                    continue

                def section(e, lst=lst):
                    for fn in lst:
                        fn(e)
                getattr(blk, name)(section)


def mk_sems(nc, ctx, names, paired=(), quad=()):
    sm = {}
    for n in names:
        if n in quad:
            sm[n] = [Sem(ctx.enter_context(nc.semaphore(n + chr(97 + i)))) for i in range(4)]
        elif n in paired:
            sm[n] = [Sem(ctx.enter_context(nc.semaphore(n + "a"))),
                     Sem(ctx.enter_context(nc.semaphore(n + "b")))]
        else:
            sm[n] = Sem(ctx.enter_context(nc.semaphore(n)))
    return sm



def tp_group(r, h, sm, blocks, ident, tensor_waits=(), vector_waits=(), evacs=None):
    assert len(blocks) <= 8
    gi = sm["sTP"].v
    r.wait("tensor", sm["sXNT"], gi - 1)
    for s_, v_ in tensor_waits:
        r.wait("tensor", s_, v_)
    bank = h["tp"][gi % 2]
    if evacs is None:
        srcs = [bb[0] for bb in blocks]
        evacs = [(j, 1, bb[1]) for j, bb in enumerate(blocks)]
    else:
        srcs = blocks
    for j, src in enumerate(srcs):
        r.op("tensor", "transpose", inc=(sm["sTP"] if j == len(srcs) - 1 else None),
             out=bank[:, j * 128:(j + 1) * 128], in_=src, identity=ident)
    r.wait("vector", sm["sTP"], sm["sTP"].v)
    for s_, v_ in vector_waits:
        r.wait("vector", s_, v_)
    for j, (s0, ns, dst) in enumerate(evacs):
        r.op("vector", "tensor_copy", inc=(sm["sXNT"] if j == len(evacs) - 1 else None),
             out=dst, in_=bank[:, s0 * 128:(s0 + ns) * 128])



def ln_tile_dma(r, cfg, h, sm, st, ti, x_ap, row0):
    k = st["ln_kd"]
    st["ln_kd"] += 1
    xst = h["xstage"][k % 3]
    if k >= 3:
        r.wait("sync", sm["sLN"], k - 2)
    sx = sm["sX"][k % 4]
    r.op("sync", "dma_start", inc=sx, inc_n=16,
         out=xst[:, :], in_=x_ap[row0 + ti * 128:row0 + ti * 128 + 128, :])


def ln_tile(r, cfg, h, sm, st, ti, x_ap, row0, dsts, evac_waits=()):
    D, NDM = cfg.D, cfg.NDM
    par = ti % 2
    if st["ln_kd"] == st["ln_k"]:
        ln_tile_dma(r, cfg, h, sm, st, ti, x_ap, row0)
    k = st["ln_k"]
    st["ln_k"] += 1
    assert st["ln_kd"] - k in (1, 2, 3)
    xpar = k % 3
    xst = h["xstage"][xpar]
    sx = sm["sX"][k % 4]
    r.wait("vector", sx, sx.v)
    if k >= 2:
        r.wait("vector", sm["sR0"], st["r0_after"][k - 2])
    r.op("vector", "tensor_reduce", out=h["srow"][:, :], in_=xst[:, :],
         axis=AX.X, op=OP.add)
    r.op("vector", "tensor_scalar", inc=sm["sVC"], out=h["mrow"][par][:, :],
         in0=h["srow"][:, :], scalar1=1.0 / D, scalar2=None, op0=OP.mult)
    st["vc_after"].append(sm["sVC"].v)
    r.wait("scalar", sx, sx.v)
    if k >= 2:
        r.wait("scalar", sm["sTP"], st["ln_tp_after"][k - 2])
    r.op("scalar", "activation", inc=sm["sSQ"],
         out=h["xn_lo_tm"][k % 2][:, :], in_=xst[:, :], func=AF.Square,
         accum_out=h["ssrow"][:, :])
    r.wait("vector", sm["sSQ"], sm["sSQ"].v)
    if k >= 2:
        r.wait("vector", sm["sTP"], st["ln_tp_after"][k - 2])
    r.op("vector", "tensor_tensor", out=h["t0"][:, :], in0=h["mrow"][par][:, :],
         in1=h["mrow"][par][:, :], op=OP.mult)
    r.op("vector", "scalar_tensor_tensor", out=h["vpe"][par][:, :], in0=h["ssrow"][:, :],
         scalar=1.0 / D, in1=h["t0"][:, :], op0=OP.mult, op1=OP.subtract)
    r.op("vector", "tensor_scalar", out=h["vpe"][par][:, :], in0=h["vpe"][par][:, :],
         scalar1=1e-5, scalar2=None, op0=OP.add)
    r.op("vector", "reciprocal", inc=sm["sVC"],
         out=h["inv"][par][:, :], in_=h["vpe"][par][:, :])
    r.wait("scalar", sm["sVC"], sm["sVC"].v)
    r.op("scalar", "activation", inc=sm["sR0"],
         out=h["r0"][par][:, :], in_=h["inv"][par][:, :], func=AF.Sqrt)
    st["r0_after"].append(sm["sR0"].v)
    r.wait("vector", sm["sR0"], sm["sR0"].v)
    r.op("vector", "tensor_tensor", out=h["t0"][:, :], in0=h["r0"][par][:, :],
         in1=h["r0"][par][:, :], op=OP.mult)
    r.op("vector", "tensor_tensor", out=h["t0"][:, :], in0=h["t0"][:, :],
         in1=h["vpe"][par][:, :], op=OP.mult)
    r.op("vector", "tensor_scalar", out=h["t0"][:, :], in0=h["t0"][:, :],
         scalar1=-0.5, scalar2=1.5, op0=OP.mult, op1=OP.add)
    r.op("vector", "tensor_tensor", out=h["rrow"][par][:, :],
         in0=h["r0"][par][:, :], in1=h["t0"][:, :], op=OP.mult)
    r.op("vector", "scalar_tensor_tensor", inc=sm["sRR"], out=h["nmr"][par][:, :],
         in0=h["mrow"][par][:, :], scalar=-1.0, in1=h["rrow"][par][:, :],
         op0=OP.mult, op1=OP.mult)
    hi_tm, lo_tm = h["xn_hi_tm"][k % 2], h["xn_lo_tm"][k % 2]
    r.wait("scalar", sm["sRR"], sm["sRR"].v)
    r.op("scalar", "activation", inc=sm["sHI"],
         out=hi_tm[:, :], in_=xst[:, :], func=AF.Identity,
         scale=h["rrow"][par][:, :], bias=h["nmr"][par][:, :])
    st["hi_after"].append(sm["sHI"].v)
    r.wait("vector", sm["sHI"], sm["sHI"].v)
    r.op("vector", "tensor_scalar", out=h["lntmp"][:, :], in0=hi_tm[:, :],
         scalar1=h["nmr"][par][:, :], scalar2=None, op0=OP.subtract)
    r.op("vector", "scalar_tensor_tensor", inc=sm["sLN"],
         out=lo_tm[:, :], in0=xst[:, :], scalar=h["rrow"][par][:, :],
         in1=h["lntmp"][:, :], op0=OP.mult, op1=OP.subtract)
    if dsts is not None:
        items = [("hi", hi_tm, c) for c in range(NDM)] + [("lo", lo_tm, c) for c in range(NDM)]
        first = True
        for g0 in range(0, len(items), 8):
            grp = items[g0:g0 + 8]
            srcs = [t[1][:, t[2] * 128:(t[2] + 1) * 128] for t in grp]
            evacs = []
            j = 0
            while j < len(grp):
                which, _, c0 = grp[j]
                n = 1
                while j + n < len(grp) and grp[j + n][0] == which:
                    n += 1
                evacs.append((j, n, dsts(which, c0, n)))
                j += n
            tw = [(sm["sLN"], sm["sLN"].v)] if first else ()
            tp_group(r, h, sm, srcs, h["ident_bf"][:, :],
                     tensor_waits=tw, vector_waits=(evac_waits if first else ()),
                     evacs=evacs)
            first = False
    st["ln_tp_after"].append(sm["sTP"].v)


def ln_handles(nc, ctx, h, cfg):
    sb = lambda name, shape, dt: ctx.enter_context(nc.sbuf_tensor(name, shape, dt))
    D = cfg.D
    h["xstage"] = [sb(f"xstage{i}", [128, D], F32) for i in range(3)]
    h["lntmp"] = sb("lntmp", [128, D], F32)
    for n in ["srow", "ssrow", "t0"]:
        h[n] = sb(n, [128, 1], F32)
    for n in ["vpe", "inv", "r0", "rrow", "mrow", "nmr"]:
        h[n] = [sb(n + str(i), [128, 1], F32) for i in range(2)]
    h["xn_hi_tm"] = [sb(f"xnhitm{i}", [128, D], BF16) for i in range(2)]
    h["xn_lo_tm"] = [sb(f"xnlotm{i}", [128, D], BF16) for i in range(2)]


def ln_state():
    return {"ln_k": 0, "ln_kd": 0, "vc_after": [], "r0_after": [], "hi_after": [],
            "ln_tp_after": []}


LN_SEMS = ["sX", "sVC", "sSQ", "sR0", "sRR", "sHI", "sLN", "sTP", "sXNT"]



def phase_b(nc, cfg, outer_ctx, h, sm, st, selfsems, attT_load, pre_fn=None):
    D, I, TOK = cfg.D, cfg.I, cfg.TOK
    NDM, NTT, NHALF, THALF = cfg.NDM, cfg.NTT, cfg.NHALF, cfg.THALF
    ICH, NIC, ECH, NE = cfg.ICH, cfg.NIC, cfg.ECH, cfg.NE
    NSUB = ICH // 128
    NEB = ECH // 128
    HDM = max(1, NDM // 2)

    x_in = nc.dram_tensor("x_own", [TOK, D], F32, kind="ExternalInput").ap()
    wg_d = nc.dram_tensor("wg", [D, I], F32, kind="ExternalInput").ap()
    wu_d = nc.dram_tensor("wu", [D, I], F32, kind="ExternalInput").ap()
    wd_d = nc.dram_tensor("wd", [I, D], F32, kind="ExternalInput").ap()
    wo_d = nc.dram_tensor("wo_nat", [D, D], F32, kind="ExternalInput").ap()
    bo_d = nc.dram_tensor("bo_t", [128, D], F32, kind="ExternalInput").ap()
    y_out = nc.dram_tensor("y", [TOK, D], F32, kind="ExternalOutput").ap()

    assert HDM * ICH <= D, "half-load must fit the xstage scratch"
    n_wload = (NDM + HDM - 1) // HDM

    with contextlib.ExitStack() as ctx:
        sb = lambda name, shape, dt: ctx.enter_context(nc.sbuf_tensor(name, shape, dt))
        ps = lambda name, shape, dt: ctx.enter_context(nc.psum_tensor(name, shape, dt))

        h["pacc"] = [ps(f"pacc{i}", [128, 512], F32) for i in range(4)]
        h["pdn"] = [ps(f"pdn{i}", [128, 512], F32) for i in range(2)]
        h["y"] = sb("yacc", [128, NTT, D], F32)

        for half in range(NHALF):
            t_row0 = half * THALF

            with contextlib.ExitStack() as fctx:
                fsb = lambda name, shape, dt: fctx.enter_context(
                    nc.sbuf_tensor(f"{name}_h{half}", shape, dt))
                h["xnt_hi"] = fsb("xnthi", [128, NTT, NDM, 128], BF16)
                h["xnt_lo"] = fsb("xntlo", [128, NTT, NDM, 128], BF16)
                h["wgu_hi"] = [fsb(f"wguhi{i}", [128, NDM, ICH], BF16) for i in range(2)]
                h["wgu_lo"] = [fsb(f"wgulo{i}", [128, NDM, ICH], BF16) for i in range(2)]
                h["wd_hi"] = [fsb(f"wdhi{i}", [128, NSUB, D], BF16) for i in range(2)]
                h["wd_lo"] = [fsb(f"wdlo{i}", [128, NSUB, D], BF16) for i in range(2)]
                h["sgall"] = fsb("sgall", [128, NTT, ICH], F32)
                h["hall"] = fsb("hall", [128, NTT, ICH], F32)
                h["habs"] = fsb("habs", [128, NTT, ICH], F32)
                h["g6"] = [fsb(f"g6_{i}", [128, NTT * (ICH // 4)], BF16) for i in range(6)]
                h["r4"] = [fsb(f"r4_{i}", [128, NTT * (ICH // 4)], BF16) for i in range(4)]
                h["hs_hi"] = fsb("hshi", [128, NTT, ICH], BF16)
                h["hs_lo"] = fsb("hslo", [128, NTT, ICH], BF16)
                h["hT_hi"] = fsb("hthi", [128, NSUB, NTT * 128], BF16)
                h["hT_lo"] = fsb("htlo", [128, NSUB, NTT * 128], BF16)

                r = Rec(selfsems)
                if half == 0 and pre_fn is not None:
                    pre_fn(r)
                for t in range(NTT):
                    r.op("sync", "dma_start", inc=sm["sYI"], inc_n=16,
                         out=h["y"][:, t, :], in_=bo_d[:, :])
                for ti in range(NTT):
                    def dsts(which, c0, n, ti=ti):
                        t_ = h["xnt_hi"] if which == "hi" else h["xnt_lo"]
                        return t_[:, ti, c0:c0 + n, :]
                    ln_tile(r, cfg, h, sm, st, ti, x_in, t_row0, dsts)
                r.play(nc)

                r = Rec(selfsems)
                ffn_pending = None

                wstage_last = {0: 0, 1: 0}

                def wload_split(loads, guard):
                    for sp_, stg, src, dhi, dlo in loads:
                        eng = "sync" if sp_ == 0 else "gpsimd"
                        sx = sm["sWDMA"][sp_]
                        r.wait(eng, sm["sSPL"], wstage_last[sp_])
                        r.op(eng, "dma_start", inc=sx, inc_n=16, out=stg, in_=src)
                        r.wait("vector", sx, sx.v)
                        if guard is not None:
                            r.wait("vector", guard[0], guard[1])
                            guard = None
                        r.op("vector", "tensor_copy", out=dhi, in_=stg)
                        r.op("vector", "scalar_tensor_tensor", inc=sm["sSPL"],
                             out=dlo, in0=stg, scalar=1.0, in1=dhi,
                             op0=OP.mult, op1=OP.subtract)
                        wstage_last[sp_] = sm["sSPL"].v

                def emit_wload(ic):
                    i0 = ic * ICH
                    for wsrc, slot, gsem in ((wg_d, 0, sm["sMMg"]),
                                             (wu_d, 1, sm["sMMu"])):
                        gval = gsem.v
                        loads = []
                        for hb in range(n_wload):
                            c0 = hb * HDM
                            ndm = min(HDM, NDM - c0)
                            src = bass.AP(wsrc.tensor, c0 * 128 * I + i0,
                                          [[I, 128], [128 * I, ndm], [1, ICH]])
                            stg = bass.AP(h["xstage"][hb % 2], 0,
                                          [[cfg.D, 128], [ICH, ndm], [1, ICH]])
                            loads.append((hb % 2, stg, src,
                                          h["wgu_hi"][slot][:, c0:c0 + ndm, :],
                                          h["wgu_lo"][slot][:, c0:c0 + ndm, :]))
                        wload_split(loads, (gsem, gval))
                    loads = []
                    gp_ = ic % 2
                    for hb in range(NSUB):
                        src = bass.AP(wd_d.tensor, (i0 + hb * 128) * D, [[D, 128], [1, D]])
                        stg = bass.AP(h["xstage"][hb % 2], 0, [[D, 128], [1, D]])
                        loads.append((hb % 2, stg, src, h["wd_hi"][gp_][:, hb, :],
                                      h["wd_lo"][gp_][:, hb, :]))
                    wload_split(loads, (sm["sDNt"], sm["sDNt"].v))
                    return sm["sSPL"].v

                spl_next = emit_wload(0)
                for ic in range(NIC):
                    g = half * NIC + ic
                    gp = ic % 2
                    spl_done = spl_next

                    for t in range(NTT):
                        gb = h["pacc"][t % 4]
                        r.wait("tensor", sm["sSPL"], spl_done - 2 - NSUB)
                        r.wait("tensor", sm["sHmul"], sm["sHmul"].v - (NTT - 1 - t))
                        for pi in range(3):
                            xs = ("xnt_hi", "xnt_lo", "xnt_hi")[pi]
                            w = h["wgu_hi"][0] if pi != 2 else h["wgu_lo"][0]
                            for c in range(NDM):
                                r.op("tensor", "matmul", gb[:, 0:ICH],
                                     h[xs][:, t, c, :], w[:, c, :],
                                     start=(pi == 0 and c == 0),
                                     stop=(pi == 2 and c == NDM - 1),
                                     inc=(sm["sMMg"] if pi == 2 and c == NDM - 1 else None))
                        r.wait("scalar", sm["sMMg"], sm["sMMg"].v)
                        r.op("scalar", "activation", inc=sm["sSG"],
                             out=h["sgall"][:, t, :], in_=gb[:, 0:ICH], func=AF.Silu)
                    for t in range(NTT):
                        ub = h["pacc"][t % 4]
                        r.wait("tensor", sm["sSPL"], spl_done - NSUB)
                        r.wait("tensor", sm["sSG"], sm["sSG"].v - (NTT - 1 - t))
                        r.wait("tensor", sm["sHmul"], sm["sHmul"].v - 3)
                        for pi in range(3):
                            xs = ("xnt_hi", "xnt_lo", "xnt_hi")[pi]
                            w = h["wgu_hi"][1] if pi != 2 else h["wgu_lo"][1]
                            for c in range(NDM):
                                r.op("tensor", "matmul", ub[:, 0:ICH],
                                     h[xs][:, t, c, :], w[:, c, :],
                                     start=(pi == 0 and c == 0),
                                     stop=(pi == 2 and c == NDM - 1),
                                     inc=(sm["sMMu"] if pi == 2 and c == NDM - 1 else None))
                        r.wait("vector", sm["sMMu"], sm["sMMu"].v)
                        r.op("vector", "tensor_tensor", inc=sm["sHmul"],
                             out=h["hall"][:, t, :], in0=h["sgall"][:, t, :], in1=ub[:, 0:ICH],
                             op=OP.mult)
                        r.wait("scalar", sm["sHmul"], sm["sHmul"].v)
                        r.wait("scalar", sm["sHspl"], sm["sHspl"].v)
                        r.op("scalar", "activation", inc=sm["sAbs"],
                             out=h["habs"][:, t, :], in_=h["hall"][:, t, :], func=AF.Abs)
                    if ffn_pending is not None:
                        ffn_pending()
                        ffn_pending = None
                    if ic + 1 < NIC:
                        spl_next = emit_wload(ic + 1)
                    r.wait("vector", sm["sAbs"], sm["sAbs"].v)

                    def plane(tns, kk):
                        return tns[:, :, kk::4]
                    g6, r4 = h["g6"], h["r4"]
                    for i, (a_, b_) in enumerate([(1, 0), (2, 0), (3, 0), (2, 1), (3, 1), (3, 2)]):
                        r.op("vector", "tensor_tensor", out=g6[i][:, :], in0=plane(h["habs"], a_),
                             in1=plane(h["habs"], b_), op=OP.is_gt)
                    r.op("vector", "tensor_tensor", out=r4[0][:, :], in0=g6[0][:, :], in1=g6[1][:, :], op=OP.add)
                    r.op("vector", "tensor_tensor", out=r4[0][:, :], in0=r4[0][:, :], in1=g6[2][:, :], op=OP.add)
                    r.op("vector", "tensor_tensor", out=r4[1][:, :], in0=g6[3][:, :], in1=g6[4][:, :], op=OP.add)
                    r.op("vector", "scalar_tensor_tensor", out=r4[1][:, :], in0=g6[0][:, :], scalar=-1.0,
                         in1=r4[1][:, :], op0=OP.mult, op1=OP.add)
                    r.op("vector", "tensor_scalar", out=r4[1][:, :], in0=r4[1][:, :], scalar1=1.0,
                         scalar2=None, op0=OP.add)
                    r.op("vector", "tensor_tensor", out=r4[2][:, :], in0=g6[1][:, :], in1=g6[3][:, :], op=OP.add)
                    r.op("vector", "scalar_tensor_tensor", out=r4[2][:, :], in0=r4[2][:, :], scalar=-1.0,
                         in1=g6[5][:, :], op0=OP.mult, op1=OP.add)
                    r.op("vector", "tensor_scalar", out=r4[2][:, :], in0=r4[2][:, :], scalar1=2.0,
                         scalar2=None, op0=OP.add)
                    r.op("vector", "tensor_tensor", out=r4[3][:, :], in0=g6[2][:, :], in1=g6[4][:, :], op=OP.add)
                    r.op("vector", "tensor_tensor", out=r4[3][:, :], in0=r4[3][:, :], in1=g6[5][:, :], op=OP.add)
                    r.op("vector", "tensor_scalar", out=r4[3][:, :], in0=r4[3][:, :], scalar1=-1.0,
                         scalar2=3.0, op0=OP.mult, op1=OP.add)
                    for kk in range(4):
                        r.op("vector", "tensor_scalar", out=r4[kk][:, :], in0=r4[kk][:, :], scalar1=1.5,
                             scalar2=None, op0=OP.is_lt)
                        r.op("vector", "tensor_tensor",
                             out=plane(h["hall"], kk),
                             in0=plane(h["hall"], kk), in1=r4[kk][:, :], op=OP.mult)
                    r.wait("vector", sm["sTP"], sm["sTP"].v)
                    r.op("vector", "tensor_copy", out=h["hs_hi"][:, :, :], in_=h["hall"][:, :, :])
                    r.op("vector", "scalar_tensor_tensor", inc=sm["sHspl"],
                         out=h["hs_lo"][:, :, :], in0=h["hall"][:, :, :], scalar=1.0,
                         in1=h["hs_hi"][:, :, :], op0=OP.mult, op1=OP.subtract)
                    def make_td(gp=gp, spl_done=spl_done):
                        hspl_v = sm["sHspl"].v
                        dnt_v = sm["sDNt"].v

                        def td():
                            items = []
                            for s_ in range(2):
                                srcs = h["hs_hi"] if s_ == 0 else h["hs_lo"]
                                dstt = h["hT_hi"] if s_ == 0 else h["hT_lo"]
                                for jj in range(NSUB):
                                    for t in range(NTT):
                                        items.append((srcs[:, t, jj * 128:(jj + 1) * 128],
                                                      dstt, jj, t))
                            first = True
                            for g0 in range(0, len(items), 8):
                                grp = items[g0:g0 + 8]
                                srcs_ = [it[0] for it in grp]
                                evacs = []
                                j = 0
                                while j < len(grp):
                                    dstt, jj, t0 = grp[j][1], grp[j][2], grp[j][3]
                                    n = 1
                                    while (j + n < len(grp) and grp[j + n][1] is dstt
                                           and grp[j + n][2] == jj and grp[j + n][3] == t0 + n):
                                        n += 1
                                    evacs.append((j, n, dstt[:, jj, t0 * 128:(t0 + n) * 128]))
                                    j += n
                                tw = [(sm["sHspl"], hspl_v)] if first else ()
                                vw = [(sm["sDNt"], dnt_v)] if first else ()
                                tp_group(r, h, sm, srcs_, h["ident_bf"][:, :],
                                         tensor_waits=tw, vector_waits=vw, evacs=evacs)
                                first = False
                            tpe_done = sm["sXNT"].v
                            r.wait("tensor", sm["sXNT"], tpe_done)
                            r.wait("tensor", sm["sSPL"], spl_done)
                            for t in range(NTT):
                                for e in range(NE):
                                    kde = sm["sDNt"].v
                                    pb = h["pdn"][kde % 2]
                                    r.wait("tensor", sm["sYA"], sm["sYA"].v - 1)
                                    for pi, (hh, ws) in enumerate((("hT_hi", "wd_hi"), ("hT_lo", "wd_hi"),
                                                                   ("hT_hi", "wd_lo"))):
                                        for j in range(NSUB):
                                            r.op("tensor", "matmul", pb[:, 0:ECH],
                                                 h[hh][:, j, t * 128:(t + 1) * 128],
                                                 h[ws][gp][:, j, e * ECH:(e + 1) * ECH],
                                                 start=(pi == 0 and j == 0),
                                                 stop=(pi == 2 and j == NSUB - 1),
                                                 inc=(sm["sDNt"] if pi == 2 and j == NSUB - 1 else None))
                                    r.wait("vector", sm["sDNt"], sm["sDNt"].v)
                                    r.wait("vector", sm["sYI"], sm["sYI"].v)
                                    r.op("vector", "tensor_tensor", inc=sm["sYA"],
                                         out=h["y"][:, t, e * ECH:(e + 1) * ECH],
                                         in0=h["y"][:, t, e * ECH:(e + 1) * ECH],
                                         in1=pb[:, 0:ECH], op=OP.add)
                        return td
                    ffn_pending = make_td()
                ffn_pending()
                ffn_pending = None
                r.play(nc)

            with contextlib.ExitStack() as wctx:
                wsb = lambda name, shape, dt: wctx.enter_context(
                    nc.sbuf_tensor(f"{name}_h{half}", shape, dt))
                h["attT_stage"] = [wsb(f"atts{i}", [128, 4, THALF], F32) for i in range(2)]
                h["attT_hi"] = wsb("atthi", [128, NDM, THALF], BF16)
                h["attT_lo"] = wsb("attlo", [128, NDM, THALF], BF16)
                h["wo_st"] = wsb("wost", [128, NEB, D], F32)
                h["wo_hi"] = wsb("wohi", [128, NDM, ECH], BF16)
                h["wo_lo"] = wsb("wolo", [128, NDM, ECH], BF16)

                r = Rec(selfsems)
                ATG = 4
                ld_eng = getattr(attT_load, "eng", "sync")
                for gi_, c0 in enumerate(range(0, NDM, ATG)):
                    nch = min(ATG, NDM - c0)
                    if gi_ >= 2:
                        r.wait(ld_eng, sm["sATs"], sm["sATs"].v - 1)
                    sat = sm["sATd"][gi_ % 2]
                    attT_load(r, c0, nch, t_row0, h["attT_stage"][gi_ % 2][:, 0:nch, :], sat)
                    r.wait("scalar", sat, sat.v)
                    r.wait("vector", sat, sat.v)
                    stg = h["attT_stage"][gi_ % 2]
                    for cc in range(nch):
                        c = c0 + cc
                        r.op("scalar", "activation", inc=sm["sSCp"],
                             out=h["attT_hi"][:, c, :], in_=stg[:, cc, :], func=AF.Identity)
                        r.wait("vector", sm["sSCp"], sm["sSCp"].v)
                        r.op("vector", "scalar_tensor_tensor",
                             inc=(sm["sATs"] if cc == nch - 1 else None),
                             out=h["attT_lo"][:, c, :], in0=stg[:, cc, :], scalar=1.0,
                             in1=h["attT_hi"][:, c, :], op0=OP.mult, op1=OP.subtract)
                ats_done = sm["sATs"].v
                for e in range(NE):
                    r.wait("sync", sm["sWoT"], sm["sWoT"].v)
                    r.op("sync", "dma_start", inc=sm["sWoD"], inc_n=16, out=h["wo_st"][:, :, :],
                         in_=bass.AP(wo_d.tensor, (e * ECH) * D, [[D, 128], [128 * D, NEB], [1, D]]))
                    r.wait("tensor", sm["sWoD"], sm["sWoD"].v)
                    womm_before = sm["sWoMM"].v
                    for f_ in range(NDM):
                        g0 = f_
                        grp = [(f_, se) for se in range(NEB)]
                        gi = sm["sWoT"].v
                        r.wait("tensor", sm["sWoTe"], gi - 1)
                        bank = h["pdn"][gi % 2]
                        for j, (f, se) in enumerate(grp):
                            r.op("tensor", "transpose", inc=(sm["sWoT"] if j == len(grp) - 1 else None),
                                 out=bank[:, j * 128:(j + 1) * 128],
                                 in_=h["wo_st"][:, se, f * 128:(f + 1) * 128],
                                 identity=h["ident_f"][:, :])
                        r.wait("vector", sm["sWoT"], sm["sWoT"].v)
                        if f_ == 0 and e > 0:
                            r.wait("vector", sm["sWoMM"], womm_before)
                        f0 = grp[0][0]
                        se0 = grp[0][1]
                        nsl = len(grp)
                        assert all(f == f0 for f, _ in grp)
                        dst_hi = h["wo_hi"][:, f0, se0 * 128:(se0 + nsl) * 128]
                        dst_lo = h["wo_lo"][:, f0, se0 * 128:(se0 + nsl) * 128]
                        r.op("vector", "tensor_copy", out=dst_hi, in_=bank[:, 0:nsl * 128])
                        r.op("vector", "scalar_tensor_tensor", inc=sm["sWoTe"],
                             out=dst_lo, in0=bank[:, 0:nsl * 128], scalar=1.0,
                             in1=dst_hi, op0=OP.mult, op1=OP.subtract)
                    wote_done = sm["sWoTe"].v
                    for t in range(NTT):
                        pb = h["pacc"][sm["sWoMM"].v % 4]
                        r.wait("tensor", sm["sWoTe"], wote_done)
                        r.wait("tensor", sm["sATs"], ats_done)
                        r.wait("tensor", sm["sWoYA"], sm["sWoYA"].v - 3)
                        for pi, (at, ws) in enumerate((("attT_hi", "wo_hi"), ("attT_lo", "wo_hi"),
                                                       ("attT_hi", "wo_lo"))):
                            for f in range(NDM):
                                r.op("tensor", "matmul", pb[:, 0:ECH],
                                     h[at][:, f, t * 128:(t + 1) * 128], h[ws][:, f, :],
                                     start=(pi == 0 and f == 0),
                                     stop=(pi == 2 and f == NDM - 1),
                                     inc=(sm["sWoMM"] if pi == 2 and f == NDM - 1 else None))
                        r.wait("vector", sm["sWoMM"], sm["sWoMM"].v)
                        r.op("vector", "tensor_tensor", inc=sm["sWoYA"],
                             out=h["y"][:, t, e * ECH:(e + 1) * ECH],
                             in0=h["y"][:, t, e * ECH:(e + 1) * ECH],
                             in1=pb[:, 0:ECH], op=OP.add)
                for t in range(NTT):
                    row = t_row0 + t * 128
                    if t >= 2:
                        r.wait("sync", sm["sYR"], sm["sYR"].v - 1)
                    sre = sm["sRes"][t % 2]
                    r.op("sync", "dma_start", inc=sre, inc_n=16,
                         out=h["xstage"][t % 2][:, :], in_=x_in[row:row + 128, :])
                    r.wait("vector", sre, sre.v)
                    r.op("vector", "tensor_tensor", inc=sm["sYR"],
                         out=h["y"][:, t, :], in0=h["y"][:, t, :], in1=h["xstage"][t % 2][:, :],
                         op=OP.add)
                    r.wait("sync", sm["sYR"], sm["sYR"].v)
                    r.op("sync", "dma_start", inc=sm["sYO"], inc_n=16,
                         out=y_out[row:row + 128, :], in_=h["y"][:, t, :])
                r.wait("sync", sm["sYO"], sm["sYO"].v)
                r.play(nc)


B_SEMS = ["sCst", "sATd", "sATs", "sWDMA", "sSPL", "sMMg", "sSG", "sMMu", "sHmul",
          "sAbs", "sHspl", "sDNt", "sYA", "sYI", "sSCp", "sMSK",
          "sWoD", "sWoT", "sWoTe", "sWoMM", "sWoYA", "sRes", "sYR", "sYO", "sCC"]


def build_ffn(cfg):
    nc = bass.Bass("TRN2", target_bir_lowering=False)
    attT = nc.dram_tensor("attT_own", [cfg.D, cfg.TOK], F32, kind="ExternalInput").ap()
    identbf_d = nc.dram_tensor("ident_bf", [128, 128], BF16, kind="ExternalInput").ap()
    identf_d = nc.dram_tensor("ident_f", [128, 128], F32, kind="ExternalInput").ap()
    with contextlib.ExitStack() as ctx:
        sb = lambda name, shape, dt: ctx.enter_context(nc.sbuf_tensor(name, shape, dt))
        ps = lambda name, shape, dt: ctx.enter_context(nc.psum_tensor(name, shape, dt))
        h = {"tp": [ps(f"tp{i}", [128, 1024], BF16) for i in range(2)],
             "ident_bf": sb("identbf", [128, 128], BF16),
             "ident_f": sb("identf", [128, 128], F32)}
        ln_handles(nc, ctx, h, cfg)
        sm = mk_sems(nc, ctx, LN_SEMS + B_SEMS, paired={"sATd", "sWDMA", "sRes"}, quad={"sX"})
        selfsems = {e: Sem(ctx.enter_context(nc.semaphore("zz_" + e))) for e in Rec.SELF_SERIAL}
        st = ln_state()
        r = Rec(selfsems)
        r.op("sync", "dma_start", inc=sm["sCst"], inc_n=16, out=h["ident_bf"][:, :], in_=identbf_d[:, :])
        r.op("sync", "dma_start", inc=sm["sCst"], inc_n=16, out=h["ident_f"][:, :], in_=identf_d[:, :])
        r.wait("sync", sm["sCst"], sm["sCst"].v)
        r.play(nc)

        def attT_load(r, c0, nch, t_row0, dst_ap, sem):
            r.op("sync", "dma_start", inc=sem, inc_n=16, out=dst_ap,
                 in_=bass.AP(attT.tensor, (c0 * 128) * cfg.TOK + t_row0,
                             [[cfg.TOK, 128], [128 * cfg.TOK, nch], [1, cfg.THALF]]))
        phase_b(nc, cfg, ctx, h, sm, st, selfsems, attT_load)
    return nc


DBG_ATT = False
DBG_ATTIN = False

def build_fused(cfg):
    nc = bass.Bass("TRN2", target_bir_lowering=False)
    B, S, D, PW = cfg.B, cfg.S, cfg.D, cfg.PW
    dbg_d = nc.dram_tensor("dbg_att", [D, B * S], F32, kind="ExternalOutput").ap() if DBG_ATT else None
    identbf_d = nc.dram_tensor("ident_bf", [128, 128], BF16, kind="ExternalInput").ap()
    identf_d = nc.dram_tensor("ident_f", [128, 128], F32, kind="ExternalInput").ap()
    tok0_d = nc.dram_tensor("tok0", [1, 1], mybir.dt.int32, kind="ExternalInput").ap()
    attT_loc = nc.dram_tensor("attT_loc", [PW, B * S], F32).ap()
    attT_full = nc.dram_tensor("attT_full", [D, B * S], F32).ap()
    with contextlib.ExitStack() as ctx:
        sb = lambda name, shape, dt: ctx.enter_context(nc.sbuf_tensor(name, shape, dt))
        ps = lambda name, shape, dt: ctx.enter_context(nc.psum_tensor(name, shape, dt))
        h = {"tp": [ps(f"tp{i}", [128, 1024], BF16) for i in range(2)],
             "ident_bf": sb("identbf", [128, 128], BF16),
             "ident_f": sb("identf", [128, 128], F32),
             "tok0": sb("tok0sb", [1, 1], mybir.dt.int32)}
        ln_handles(nc, ctx, h, cfg)
        all_sems = LN_SEMS + A_SEMS + [x for x in B_SEMS if x not in A_SEMS]
        sm = mk_sems(nc, ctx, all_sems,
                     paired={"sQTd", "sAO", "sATd", "sWDMA", "sRes"}, quad={"sX"})
        selfsems = {e: Sem(ctx.enter_context(nc.semaphore("zz_" + e))) for e in Rec.SELF_SERIAL}
        st = ln_state()
        r = Rec(selfsems)
        r.op("sync", "dma_start", inc=sm["sCst"], inc_n=16, out=h["ident_bf"][:, :], in_=identbf_d[:, :])
        r.op("sync", "dma_start", inc=sm["sCst"], inc_n=16, out=h["ident_f"][:, :], in_=identf_d[:, :])
        r.op("sync", "dma_start", inc=sm["sCst"], inc_n=16, out=h["tok0"][:, :], in_=tok0_d[:, :])
        r.wait("sync", sm["sCst"], sm["sCst"].v)
        r.play(nc)

        phase_a(nc, cfg, ctx, h, sm, st, selfsems, attT_loc)
        ao_fin = (sm["sAO"][0].v, sm["sAO"][1].v)

        def pre_fn(r):
            r.wait("gpsimd", sm["sAO"][0], ao_fin[0])
            r.wait("gpsimd", sm["sAO"][1], ao_fin[1])
            r.op("gpsimd", "collective_compute", "AllGather", OP.bypass,
                 [[i for i in range(cfg.n_cores)]],
                 [attT_loc.opt()], [attT_full.opt()], inc=sm["sCC"])
            if DBG_ATT:
                r.wait("gpsimd", sm["sCC"], 1)
                r.op("gpsimd", "dma_start", inc=sm["sCC"], inc_n=16,
                     out=dbg_d[:, :], in_=attT_full[:, :])
                r.wait("gpsimd", sm["sCC"], 17)

        regstate = {}
        attT_own_d = (nc.dram_tensor("attT_own", [D, cfg.TOK], F32, kind="ExternalInput").ap()
                      if DBG_ATTIN else None)

        def attT_load(r, c0, nch, t_row0, dst_ap, sem):
            if DBG_ATTIN:
                r.op("sync", "dma_start", inc=sem, inc_n=16, out=dst_ap,
                     in_=bass.AP(attT_own_d.tensor, (c0 * 128) * cfg.TOK + t_row0,
                                 [[cfg.TOK, 128], [128 * cfg.TOK, nch], [1, cfg.THALF]]))
                return
            r.wait("gpsimd", sm["sCC"], 1)

            def fn(e, r=r, c0=c0, nch=nch, t_row0=t_row0, dst_ap=dst_ap, s=sem):
                key = t_row0 // cfg.THALF
                regs = regstate.setdefault(key, {})
                if "r_tok" not in regs:
                    regs["r_tok"] = e.alloc_register(f"r_tok{key}")
                    regs["r_off"] = e.alloc_register(f"r_off{key}")
                    e.reg_load(regs["r_tok"], h["tok0"][0:1, 0:1])
                e.reg_add(regs["r_off"], regs["r_tok"], (c0 * 128) * (B * S) + t_row0)
                e.dma_start(out=dst_ap,
                            in_=bass.AP(attT_full.tensor, regs["r_off"],
                                        [[B * S, 128], [128 * B * S, nch],
                                         [1, cfg.THALF]])).then_inc(s.h, 16)
            sem.v += 16
            r.emit("gpsimd", fn)
        attT_load.eng = "sync" if DBG_ATTIN else "gpsimd"
        phase_b(nc, cfg, ctx, h, sm, st, selfsems, attT_load, pre_fn=pre_fn)
    return nc



def _chunks_of(qt):
    ext = 128 * (qt + 1)
    out, s = [], 0
    while s < ext:
        w = min(512, ext - s)
        out.append((s, w))
        s += w
    return out


def _stripes_of(qt):
    ch = _chunks_of(qt)
    return [ch[i:i + STRIPE_CHUNKS] for i in range(0, len(ch), STRIPE_CHUNKS)]


def phase_a(nc, cfg, outer_ctx, h, sm, st, selfsems, attT_d):
    B, S, D = cfg.B, cfg.S, cfg.D
    NDM, NT, HPC, PW = cfg.NDM, cfg.NT, cfg.HPC, cfg.PW
    x_d = nc.dram_tensor("x", [B * S, D], F32, kind="ExternalInput").ap()
    wq_d = nc.dram_tensor("wq_nat", [PW, D], F32, kind="ExternalInput").ap()
    wk_d = nc.dram_tensor("wk_nat", [PW, D], F32, kind="ExternalInput").ap()
    wv_d = nc.dram_tensor("wv_nat", [PW, D], F32, kind="ExternalInput").ap()
    bq_d = nc.dram_tensor("bq_t", [128, PW], F32, kind="ExternalInput").ap()
    bk_d = nc.dram_tensor("bk_t", [128, PW], F32, kind="ExternalInput").ap()
    bv_d = nc.dram_tensor("bv_t", [128, PW], F32, kind="ExternalInput").ap()
    chi_d = nc.dram_tensor("cos_hi", [S, 64], F32, kind="ExternalInput").ap()
    clo_d = nc.dram_tensor("cos_lo", [S, 64], F32, kind="ExternalInput").ap()
    shi_d = nc.dram_tensor("sin_hi", [S, 64], F32, kind="ExternalInput").ap()
    slo_d = nc.dram_tensor("sin_lo", [S, 64], F32, kind="ExternalInput").ap()
    mask_d = nc.dram_tensor("bigmask", [128, 512], F32, kind="ExternalInput").ap()
    qT_scr = nc.dram_tensor("qT_scr", [B * HPC * 2 * 128, S], BF16).ap()

    sc64 = np.float64(1.0) / np.sqrt(np.float64(128.0))
    sc_hi = float(np.float32(sc64))
    sc_lo = float(np.float32(sc64 - np.float64(np.float32(sc64))))

    with contextlib.ExitStack() as ctx:
        sb = lambda name, shape, dt: ctx.enter_context(nc.sbuf_tensor(name, shape, dt))
        ps = lambda name, shape, dt: ctx.enter_context(nc.psum_tensor(name, shape, dt))
        h["scA"] = ps("scA", [128, 1536], F32)
        h["scB"] = ps("scB", [128, 1536], F32)
        h["mask"] = sb("mask", [128, 512], F32)
        h["cc"] = sb("cc", [128, NT, 64], F32)
        h["ss"] = sb("ss", [128, NT, 64], F32)
        h["bq"] = sb("bq", [1, PW], F32)
        h["bk"] = sb("bk", [1, PW], F32)
        h["bv"] = sb("bv", [1, PW], F32)
        for nm in ("wq", "wk", "wv"):
            h[nm + "_hi"] = sb(nm + "hi", [128, NDM, PW], BF16)
            h[nm + "_lo"] = sb(nm + "lo", [128, NDM, PW], BF16)
        h["xnt_st_hi"] = sb("xntsth", [128, NDM, 128], BF16)
        h["xnt_st_lo"] = sb("xntstl", [128, NDM, 128], BF16)
        h["v_hi"] = sb("vhi", [128, NT, PW], BF16)
        h["v_lo"] = sb("vlo", [128, NT, PW], BF16)
        h["kT_hi"] = sb("kthi", [128, HPC, S], BF16)
        h["kT_lo"] = sb("ktlo", [128, HPC, S], BF16)
        h["prot"] = sb("prot", [128, PW], F32)
        h["ones"] = sb("ones", [1, 128], BF16)
        h["bhl"] = sb("bhl", [1, 6, PW], BF16)
        h["phi"] = [sb(f"phi{i}", [128, PW], BF16) for i in range(2)]
        h["plo"] = [sb(f"plo{i}", [128, PW], BF16) for i in range(2)]
        hw_ = max(128, PW // 2)
        h["rtmp"] = [h["lntmp"][:, 0:hw_], h["lntmp"][:, hw_:2 * hw_]]
        h["qsend"] = sb("qsend", [128, HPC * 2 * 128], BF16)
        h["qt_hi"] = [sb(f"qthi{i}", [128, 128], BF16) for i in range(2)]
        h["qt_lo"] = [sb(f"qtlo{i}", [128, 128], BF16) for i in range(2)]
        h["p_hi"] = sb("p_hi", [128, 1536], BF16)
        h["p_lo"] = sb("p_lo", [128, 1536], BF16)
        h["pT_hi"] = sb("pThi", [128, 4, 128], BF16)
        h["pT_lo"] = sb("pTlo", [128, 4, 128], BF16)
        h["negm"] = [sb(f"negm{i}", [128, 1], F32) for i in range(2)]
        h["nmrun"] = [sb(f"nmrun{i}", [128, 1], F32) for i in range(2)]
        h["dm"] = sb("dm", [128, 1], F32)
        h["resc"] = sb("resc", [128, 1], F32)
        h["ls"] = sb("ls", [128, 1], F32)
        h["ncm"] = sb("ncm", [128, 1], F32)
        h["lsc"] = sb("lsc", [128, 4], F32)
        h["lrun"] = sb("lrun", [128, 1], F32)
        h["rl"] = sb("rl", [128, 1], F32)
        h["attacc"] = sb("attacc", [128, 128], F32)
        h["attst"] = sb("attst", [128, 128], F32)

        r = Rec(selfsems)
        r.op("sync", "dma_start", inc=sm["sCst"], inc_n=16, out=h["mask"][:, :], in_=mask_d[:, :])
        for dst, src in (("bq", bq_d), ("bk", bk_d), ("bv", bv_d)):
            r.op("sync", "dma_start", inc=sm["sCst"], inc_n=16, out=h[dst][:, :], in_=src[0:1, :])
        tbl_ap = lambda d: bass.AP(d.tensor, 0, [[64, 128], [128 * 64, NT], [1, 64]])
        if NT * 64 <= cfg.D:
            xs0 = lambda: bass.AP(h["xstage"][0], 0, [[cfg.D, 128], [64, NT], [1, 64]])
        else:
            h["cstage"] = sb("cstage", [128, NT, 64], F32)
            xs0 = lambda: h["cstage"][:, :, :]
        r.op("sync", "dma_start", inc=sm["sCst"], inc_n=16, out=h["cc"][:, :, :], in_=tbl_ap(chi_d))
        r.op("sync", "dma_start", inc=sm["sCst"], inc_n=16, out=xs0(), in_=tbl_ap(clo_d))
        r.wait("vector", sm["sCst"], sm["sCst"].v)
        r.op("vector", "tensor_tensor", inc=sm["sWe"], out=h["cc"][:, :, :], in0=h["cc"][:, :, :],
             in1=xs0(), op=OP.add)
        r.op("sync", "dma_start", inc=sm["sCst2"], inc_n=16, out=h["ss"][:, :, :], in_=tbl_ap(shi_d))
        r.wait("sync", sm["sWe"], 1)
        r.op("sync", "dma_start", inc=sm["sCst2"], inc_n=16, out=xs0(), in_=tbl_ap(slo_d))
        r.wait("vector", sm["sCst2"], sm["sCst2"].v)
        r.op("vector", "tensor_tensor", inc=sm["sWe"], out=h["ss"][:, :, :], in0=h["ss"][:, :, :],
             in1=xs0(), op=OP.add)
        r.op("vector", "tensor_scalar", out=h["lntmp"][0:1, 0:PW], in0=h["bq"][:, :],
             scalar1=sc_lo, scalar2=None, op0=OP.mult)
        r.op("vector", "tensor_scalar", out=h["bq"][:, :], in0=h["bq"][:, :], scalar1=sc_hi,
             scalar2=None, op0=OP.mult)
        r.op("vector", "tensor_tensor", out=h["bq"][:, :], in0=h["bq"][:, :],
             in1=h["lntmp"][0:1, 0:PW], op=OP.add)
        r.op("vector", "memset", h["ones"][0:1, :], 1.0)
        for bi, bn in enumerate(("bq", "bk", "bv")):
            r.op("vector", "tensor_copy", out=h["bhl"][0:1, 2 * bi, :], in_=h[bn][0:1, :])
            r.op("vector", "scalar_tensor_tensor", out=h["bhl"][0:1, 2 * bi + 1, :],
                 in0=h[bn][0:1, :], scalar=1.0, in1=h["bhl"][0:1, 2 * bi, :],
                 op0=OP.mult, op1=OP.subtract)

        we_after = []
        wt_after = []
        slot = 0
        wnat = h["xstage"][1]
        for nm, wdd in (("wq", wq_d), ("wk", wk_d), ("wv", wv_d)):
            for rb in range(PW // 128):
                j = len(wt_after)
                if j >= 1:
                    r.wait("sync", sm["sWt"], wt_after[j - 1])
                r.op("sync", "dma_start", inc=sm["sWn"], inc_n=16,
                     out=wnat[:, :], in_=wdd[rb * 128:(rb + 1) * 128, :])
                r.wait("tensor", sm["sWn"], sm["sWn"].v)
                for c in range(NDM):
                    if slot >= 2:
                        r.wait("tensor", sm["sWe"], we_after[slot - 2])
                    bank = h["scA"] if slot % 2 == 0 else h["scB"]
                    r.op("tensor", "transpose", inc=sm["sWt"],
                         out=bank[:, 0:128], in_=wnat[:, c * 128:(c + 1) * 128],
                         identity=h["ident_f"][:, :])
                    r.wait("vector", sm["sWt"], sm["sWt"].v)
                    dhi = h[nm + "_hi"][:, c, rb * 128:(rb + 1) * 128]
                    dlo = h[nm + "_lo"][:, c, rb * 128:(rb + 1) * 128]
                    if nm == "wq":
                        r.op("vector", "tensor_scalar", out=h["rtmp"][0][:, 0:128], in0=bank[:, 0:128],
                             scalar1=sc_hi, scalar2=None, op0=OP.mult)
                        r.op("vector", "scalar_tensor_tensor", out=h["rtmp"][1][:, 0:128],
                             in0=bank[:, 0:128], scalar=sc_lo, in1=h["rtmp"][0][:, 0:128],
                             op0=OP.mult, op1=OP.add)
                        r.op("vector", "tensor_copy", out=dhi, in_=h["rtmp"][1][:, 0:128])
                        r.op("vector", "scalar_tensor_tensor", inc=sm["sWe"],
                             out=dlo, in0=h["rtmp"][1][:, 0:128], scalar=1.0, in1=dhi,
                             op0=OP.mult, op1=OP.subtract)
                    else:
                        r.op("vector", "tensor_copy", out=dhi, in_=bank[:, 0:128])
                        r.op("vector", "scalar_tensor_tensor", inc=sm["sWe"],
                             out=dlo, in0=bank[:, 0:128], scalar=1.0, in1=dhi,
                             op0=OP.mult, op1=OP.subtract)
                    we_after.append(sm["sWe"].v)
                    slot += 1
                wt_after.append(sm["sWt"].v)
        r.play(nc)

        for b in range(B):
            r = Rec(selfsems)
            prj_after = []
            kt_after = []
            epi_pending = None

            def emit_ln(ti):
                def dsts(which, c0, n):
                    t_ = h["xnt_st_hi"] if which == "hi" else h["xnt_st_lo"]
                    return t_[:, c0:c0 + n, :]
                ev = []
                if ti >= 1:
                    ev = [(sm["sPRJ"], prj_after[ti - 1])]
                ln_tile(r, cfg, h, sm, st, ti, x_d, b * S, dsts, evac_waits=ev)
                return sm["sXNT"].v

            xnt_done_next = emit_ln(0)
            if NT > 1:
                ln_tile_dma(r, cfg, h, sm, st, 1, x_d, b * S)
            for ti in range(NT):
                par = ti % 2
                xnt_done = xnt_done_next
                bank = h["scA"] if ti % 2 == 0 else h["scB"]
                r.wait("tensor", sm["sXNT"], xnt_done)
                r.wait("tensor", sm["sEPI"], sm["sEPI"].v)
                for wi, nm in enumerate(("wq", "wk", "wv")):
                    out = bank[:, wi * 512:wi * 512 + PW]
                    for pi, xs in enumerate(("hi", "lo", "hi")):
                        xn = h["xnt_st_hi"] if xs == "hi" else h["xnt_st_lo"]
                        w = h[nm + "_hi"] if pi != 2 else h[nm + "_lo"]
                        for c in range(NDM):
                            r.op("tensor", "matmul", out,
                                 xn[:, c, :], w[:, c, :],
                                 start=(pi == 0 and c == 0), stop=False)
                    r.op("tensor", "matmul", out, h["ones"][0:1, :], h["bhl"][0:1, 2 * wi, :],
                         start=False, stop=False)
                    r.op("tensor", "matmul", out, h["ones"][0:1, :], h["bhl"][0:1, 2 * wi + 1, :],
                         start=False, stop=True,
                         inc=(sm["sPRJ"] if wi == 2 else None))
                prj_after.append(sm["sPRJ"].v)

                if ti + 2 < NT:
                    ln_tile_dma(r, cfg, h, sm, st, ti + 2, x_d, b * S)
                if ti + 1 < NT:
                    xnt_done_next = emit_ln(ti + 1)
                if epi_pending is not None:
                    epi_pending()

                def make_epi(ti=ti, par=par, bank=bank, prjv=sm["sPRJ"].v):
                    def epi():
                        k_epi = len(kt_after)
                        r.wait("vector", sm["sPRJ"], prjv)
                        if k_epi >= 2:
                            r.wait("vector", sm["sTP"], kt_after[k_epi - 2])
                        rot = h["prot"]
                        for nm in ("q", "k", "v"):
                            wi = {"q": 0, "k": 1, "v": 2}[nm]
                            src_ = bank[:, wi * 512:wi * 512 + PW]
                            if nm in ("q", "k"):
                                e_pl = bank[:, wi * 512:wi * 512 + PW:2]
                                o_pl = bank[:, wi * 512 + 1:wi * 512 + PW:2]
                                cc = bass.AP(h["cc"], ti * 64, [[NT * 64, 128], [0, HPC], [1, 64]])
                                ss_ = bass.AP(h["ss"], ti * 64, [[NT * 64, 128], [0, HPC], [1, 64]])
                                re, ro = rot[:, 0:PW:2], rot[:, 1:PW:2]
                                t0_, t1_ = h["rtmp"][0][:, 0:PW // 2], h["rtmp"][1][:, 0:PW // 2]
                                r.op("vector", "tensor_tensor", out=t0_, in0=e_pl, in1=cc, op=OP.mult)
                                r.op("vector", "tensor_tensor", out=t1_, in0=o_pl, in1=ss_, op=OP.mult)
                                r.op("vector", "tensor_tensor", out=re, in0=t0_, in1=t1_, op=OP.subtract)
                                r.op("vector", "tensor_tensor", out=t0_, in0=e_pl, in1=ss_, op=OP.mult)
                                r.op("vector", "tensor_tensor", out=t1_, in0=o_pl, in1=cc, op=OP.mult)
                                r.op("vector", "tensor_tensor", out=ro, in0=t0_, in1=t1_, op=OP.add)
                                src_f = rot[:, :]
                            else:
                                src_f = src_
                            if nm == "v":
                                dhi, dlo = h["v_hi"][:, ti, :], h["v_lo"][:, ti, :]
                                r.op("vector", "tensor_copy", out=dhi, in_=src_f)
                                r.op("vector", "scalar_tensor_tensor", inc=sm["sEPI"],
                                     out=dlo, in0=src_f, scalar=1.0, in1=dhi,
                                     op0=OP.mult, op1=OP.subtract)
                                continue
                            dhi, dlo = h["phi"][par][:, :], h["plo"][par][:, :]
                            r.op("vector", "tensor_copy", out=dhi, in_=src_f)
                            r.op("vector", "scalar_tensor_tensor", inc=sm["sKSP"],
                                 out=dlo, in0=src_f, scalar=1.0, in1=dhi,
                                 op0=OP.mult, op1=OP.subtract)
                            if nm == "q":
                                srcs = []
                                for hd in range(HPC):
                                    srcs.append(h["phi"][par][:, hd * 128:(hd + 1) * 128])
                                    srcs.append(h["plo"][par][:, hd * 128:(hd + 1) * 128])
                                sqs = sm["sQSd"]
                                assert len(srcs) <= 8
                                tp_group(r, h, sm, srcs, h["ident_bf"][:, :],
                                         tensor_waits=[(sm["sKSP"], sm["sKSP"].v)],
                                         vector_waits=[(sqs, sqs.v)],
                                         evacs=[(0, len(srcs), h["qsend"][:, 0:len(srcs) * 128])])
                                r.wait("sync", sm["sXNT"], sm["sXNT"].v)
                                dst = bass.AP(qT_scr.tensor,
                                              (b * HPC * 2 * 128) * S + ti * 128,
                                              [[S, 128], [128 * S, HPC * 2], [1, 128]])
                                r.op("sync", "dma_start", inc=sqs, inc_n=16,
                                     out=dst, in_=h["qsend"][:, :])
                            else:
                                srcs = ([h["phi"][par][:, hd * 128:(hd + 1) * 128] for hd in range(HPC)]
                                        + [h["plo"][par][:, hd * 128:(hd + 1) * 128] for hd in range(HPC)])
                                assert len(srcs) <= 8
                                evacs = [(0, HPC, h["kT_hi"][:, :, ti * 128:(ti + 1) * 128]),
                                         (HPC, HPC, h["kT_lo"][:, :, ti * 128:(ti + 1) * 128])]
                                tp_group(r, h, sm, srcs, h["ident_bf"][:, :],
                                         tensor_waits=[(sm["sKSP"], sm["sKSP"].v)], evacs=evacs)
                        kt_after.append(sm["sTP"].v)
                    return epi
                epi_pending = make_epi()
            epi_pending()
            epi_pending = None
            r.wait("sync", sm["sQSd"], sm["sQSd"].v)
            r.play(nc)

            r = Rec(selfsems)
            probs_tp_guard = 0
            probs_lo_guard = 0
            fin_pending = None
            for u in range(HPC):
                for qt in range(NT):
                    stripes = _stripes_of(qt)
                    nstr = len(stripes)
                    qp = qt % 2
                    sq = sm["sQTd"][qp]
                    r.wait("sync", sm["sSC"], sm["sSC"].v - 1)
                    base = (b * HPC * 2 + u * 2) * 128
                    r.op("sync", "dma_start", inc=sq, inc_n=16,
                         out=h["qt_hi"][qp][:, :],
                         in_=bass.AP(qT_scr.tensor, base * S + qt * 128, [[S, 128], [1, 128]]))
                    r.op("sync", "dma_start", inc=sq, inc_n=16,
                         out=h["qt_lo"][qp][:, :],
                         in_=bass.AP(qT_scr.tensor, (base + 128) * S + qt * 128, [[S, 128], [1, 128]]))
                    sc_after = {}

                    def emit_qk(si):
                        chunks = stripes[si]
                        bank = h["scA"] if si % 2 == 0 else h["scB"]
                        r.wait("tensor", sq, sq.v)
                        r.wait("tensor", sm["sFOLD"], sm["sFOLD"].v)
                        r.wait("tensor", sm["sATe"], sm["sATe"].v)
                        for cj, (k0, w) in enumerate(chunks):
                            for pi, (qb, kb) in enumerate((("qt_hi", "kT_hi"), ("qt_hi", "kT_lo"),
                                                           ("qt_lo", "kT_hi"))):
                                r.op("tensor", "matmul", bank[:, cj * 512:cj * 512 + w],
                                     h[qb][qp][:, :], h[kb][:, u, k0:k0 + w],
                                     start=(pi == 0), stop=(pi == 2),
                                     inc=(sm["sSC"] if pi == 2 else None))
                        sc_after[si] = sm["sSC"].v - len(chunks)

                    emit_qk(0)
                    if fin_pending is not None:
                        fin_pending()
                        fin_pending = None
                    for si, chunks in enumerate(stripes):
                        bank = h["scA"] if si % 2 == 0 else h["scB"]
                        str_w = sum(w for (_, w) in chunks)
                        nsub = str_w // 128
                        if si + 1 < nstr:
                            emit_qk(si + 1)
                        sp = si % 2
                        nch = len(chunks)
                        for cj, (k0, w) in enumerate(chunks):
                            r.wait("vector", sm["sSC"], sc_after[si] + cj + 1)
                            if si == nstr - 1 and cj == nch - 1:
                                r.op("vector", "tensor_tensor",
                                     out=bank[:, cj * 512:cj * 512 + w],
                                     in0=bank[:, cj * 512:cj * 512 + w],
                                     in1=h["mask"][:, 512 - w:512], op=OP.add)
                            tgt = h["negm"][sp] if cj == 0 else h["ncm"]
                            r.op("vector", "tensor_reduce", out=tgt[:, :],
                                 in_=bank[:, cj * 512:cj * 512 + w], axis=AX.X,
                                 op=OP.max, negate=True)
                            if cj > 0:
                                r.op("vector", "tensor_tensor", out=h["negm"][sp][:, :],
                                     in0=h["negm"][sp][:, :], in1=h["ncm"][:, :], op=OP.min)
                        if si == 0:
                            r.op("vector", "tensor_copy", inc=sm["sNM"],
                                 out=h["nmrun"][0][:, :], in_=h["negm"][sp][:, :])
                        else:
                            prev, cur = (si - 1) % 2, si % 2
                            r.op("vector", "tensor_tensor", out=h["nmrun"][cur][:, :],
                                 in0=h["nmrun"][prev][:, :], in1=h["negm"][sp][:, :], op=OP.min)
                            r.op("vector", "tensor_tensor", inc=sm["sNM"],
                                 out=h["dm"][:, :], in0=h["nmrun"][cur][:, :],
                                 in1=h["nmrun"][prev][:, :], op=OP.subtract)
                        nm_cur = h["nmrun"][si % 2]
                        r.wait("scalar", sm["sNM"], sm["sNM"].v)
                        if si > 0:
                            r.op("scalar", "activation", out=h["resc"][:, :], in_=h["dm"][:, :],
                                 func=AF.Exp)
                        r.wait("scalar", sm["sLO"], probs_lo_guard)
                        r.wait("scalar", sm["sTP"], probs_tp_guard)
                        exp_base = sm["sEXP"].v
                        for cj, (k0, w) in enumerate(chunks):
                            acc = h["ls"][:, :] if nch == 1 else h["lsc"][:, cj:cj + 1]
                            r.op("scalar", "activation", out=bank[:, cj * 512:cj * 512 + w],
                                 in_=bank[:, cj * 512:cj * 512 + w], func=AF.Exp,
                                 bias=nm_cur[:, :], scale=1.0, accum_out=acc)
                            r.op("scalar", "activation", inc=sm["sEXP"],
                                 out=h["p_hi"][:, cj * 512:cj * 512 + w],
                                 in_=bank[:, cj * 512:cj * 512 + w], func=AF.Copy)
                        att_slot = bank[:, 0:128]
                        pv_before = sm["sPV"].v
                        for cj, (k0, w) in enumerate(chunks):
                            csub = w // 128
                            r.wait("vector", sm["sEXP"], exp_base + cj + 1)
                            r.op("vector", "scalar_tensor_tensor",
                                 out=h["p_lo"][:, cj * 512:cj * 512 + w],
                                 in0=bank[:, cj * 512:cj * 512 + w], scalar=1.0,
                                 in1=h["p_hi"][:, cj * 512:cj * 512 + w],
                                 op0=OP.mult, op1=OP.subtract)
                            srcs = [h["p_hi"][:, (cj * 4 + j2) * 128:(cj * 4 + j2 + 1) * 128]
                                    for j2 in range(csub)]
                            tw = [(sm["sEXP"], exp_base + cj + 1)]
                            vw = [(sm["sPV"], pv_before)] if cj == 0 else ()
                            tp_group(r, h, sm, srcs, h["ident_bf"][:, :],
                                     tensor_waits=tw, vector_waits=vw,
                                     evacs=[(0, csub, h["pT_hi"][:, 0:csub, :])])
                            r.wait("tensor", sm["sXNT"], sm["sXNT"].v)
                            for pi, vv in enumerate(("v_hi", "v_lo")):
                                for j2 in range(csub):
                                    vt = (si * STRIPE_CHUNKS * 4) + cj * 4 + j2
                                    r.op("tensor", "matmul", att_slot,
                                         h["pT_hi"][:, j2, :],
                                         h[vv][:, vt, u * 128:(u + 1) * 128],
                                         start=(cj == 0 and pi == 0 and j2 == 0),
                                         stop=False)
                        r.wait("vector", sm["sEXP"], exp_base + nch)
                        if nch > 1:
                            r.op("vector", "tensor_tensor", out=h["ls"][:, :],
                                 in0=h["lsc"][:, 0:1], in1=h["lsc"][:, 1:2], op=OP.add)
                            for cj in range(2, nch):
                                r.op("vector", "tensor_tensor", out=h["ls"][:, :],
                                     in0=h["ls"][:, :], in1=h["lsc"][:, cj:cj + 1], op=OP.add)
                        if si == 0:
                            r.op("vector", "tensor_copy", inc=sm["sLO"],
                                 out=h["lrun"][:, :], in_=h["ls"][:, :])
                        else:
                            r.op("vector", "scalar_tensor_tensor", inc=sm["sLO"],
                                 out=h["lrun"][:, :], in0=h["lrun"][:, :],
                                 scalar=h["resc"][:, :], in1=h["ls"][:, :],
                                 op0=OP.mult, op1=OP.add)
                        probs_lo_guard = sm["sLO"].v
                        for r0_ in range(0, nsub, 4):
                            nblk = min(4, nsub - r0_)
                            srcs = [h["p_lo"][:, jj * 128:(jj + 1) * 128]
                                    for jj in range(r0_, r0_ + nblk)]
                            tp_group(r, h, sm, srcs, h["ident_bf"][:, :],
                                     tensor_waits=[(sm["sLO"], probs_lo_guard)],
                                     evacs=[(0, nblk, h["pT_lo"][:, 0:nblk, :])])
                            r.wait("tensor", sm["sXNT"], sm["sXNT"].v)
                            for j2 in range(nblk):
                                jj = r0_ + j2
                                vt = (si * STRIPE_CHUNKS * 4) + jj
                                last = (jj == nsub - 1)
                                r.op("tensor", "matmul", att_slot,
                                     h["pT_lo"][:, j2, :],
                                     h["v_hi"][:, vt, u * 128:(u + 1) * 128],
                                     start=False, stop=last,
                                     inc=(sm["sPV"] if last else None))
                        probs_tp_guard = sm["sTP"].v
                        r.wait("vector", sm["sPV"], sm["sPV"].v)
                        if si == 0:
                            r.op("vector", "tensor_copy", inc=sm["sFOLD"],
                                 out=h["attacc"][:, :], in_=att_slot)
                        else:
                            r.op("vector", "scalar_tensor_tensor", inc=sm["sFOLD"],
                                 out=h["attacc"][:, :], in0=h["attacc"][:, :],
                                 scalar=h["resc"][:, :], in1=att_slot, op0=OP.mult, op1=OP.add)
                    sa = sm["sAO"][qp]
                    r.wait("vector", sm["sAO"][0], sm["sAO"][0].v)
                    r.wait("vector", sm["sAO"][1], sm["sAO"][1].v)
                    r.op("vector", "reciprocal", out=h["rl"][:, :], in_=h["lrun"][:, :])
                    r.op("vector", "tensor_scalar", inc=sm["sANRM"], out=h["attst"][:, :],
                         in0=h["attacc"][:, :], scalar1=h["rl"][:, :], scalar2=None, op0=OP.mult)

                    def make_fin(u=u, qt=qt, qp=qp, sa=sa, anrm=sm["sANRM"].v):
                        def fin():
                            r.wait("tensor", sm["sANRM"], anrm)
                            r.op("tensor", "transpose", inc=sm["sATT"],
                                 out=h["scB"][:, 128:256], in_=h["attst"][:, :],
                                 identity=h["ident_f"][:, :])
                            r.wait("vector", sm["sATT"], sm["sATT"].v)
                            r.op("vector", "tensor_copy", inc=sm["sATe"],
                                 out=h["attst"][:, :], in_=h["scB"][:, 128:256])
                            r.wait("sync", sm["sATe"], sm["sATe"].v)
                            r.op("sync", "dma_start", inc=sa, inc_n=16,
                                 out=attT_d[u * 128:(u + 1) * 128,
                                            b * S + qt * 128:b * S + qt * 128 + 128],
                                 in_=h["attst"][:, :])
                        return fin
                    fin_pending = make_fin()
            fin_pending()
            fin_pending = None
            r.wait("sync", sm["sAO"][0], sm["sAO"][0].v)
            r.wait("sync", sm["sAO"][1], sm["sAO"][1].v)
            r.play(nc)


A_SEMS = ["sCst", "sCst2", "sWn", "sWt", "sWe", "sPRJ", "sEPI", "sKSP",
          "sQSd", "sQTd", "sSC", "sNM", "sEXP", "sLO", "sPV", "sFOLD",
          "sANRM", "sATT", "sATe", "sAO"]


def build_attn(cfg):
    nc = bass.Bass("TRN2", target_bir_lowering=False)
    attT_d = nc.dram_tensor("attT", [cfg.PW, cfg.B * cfg.S], F32, kind="ExternalOutput").ap()
    identbf_d = nc.dram_tensor("ident_bf", [128, 128], BF16, kind="ExternalInput").ap()
    identf_d = nc.dram_tensor("ident_f", [128, 128], F32, kind="ExternalInput").ap()
    with contextlib.ExitStack() as ctx:
        sb = lambda name, shape, dt: ctx.enter_context(nc.sbuf_tensor(name, shape, dt))
        ps = lambda name, shape, dt: ctx.enter_context(nc.psum_tensor(name, shape, dt))
        h = {"tp": [ps(f"tp{i}", [128, 1024], BF16) for i in range(2)],
             "ident_bf": sb("identbf", [128, 128], BF16),
             "ident_f": sb("identf", [128, 128], F32)}
        ln_handles(nc, ctx, h, cfg)
        sm = mk_sems(nc, ctx, LN_SEMS + A_SEMS, paired={"sQTd", "sAO"}, quad={"sX"})
        selfsems = {e: Sem(ctx.enter_context(nc.semaphore("zz_" + e))) for e in Rec.SELF_SERIAL}
        st = ln_state()
        r = Rec(selfsems)
        r.op("sync", "dma_start", inc=sm["sCst"], inc_n=16, out=h["ident_bf"][:, :], in_=identbf_d[:, :])
        r.op("sync", "dma_start", inc=sm["sCst"], inc_n=16, out=h["ident_f"][:, :], in_=identf_d[:, :])
        r.wait("sync", sm["sCst"], sm["sCst"].v)
        r.play(nc)
        phase_a(nc, cfg, ctx, h, sm, st, selfsems, attT_d)
    return nc



def _rope_tables(S):
    half = 64
    inv = 1.0 / 10000.0 ** (np.arange(half, dtype=np.float64) / half)
    ang = np.arange(S, dtype=np.float64)[:, None] * inv[None, :]
    c, s = np.cos(ang), np.sin(ang)
    chi = c.astype(np.float32)
    clo = (c - chi.astype(np.float64)).astype(np.float32)
    shi = s.astype(np.float32)
    slo = (s - shi.astype(np.float64)).astype(np.float32)
    return chi, clo, shi, slo


def _bigmask():
    m = np.zeros((128, 512), dtype=np.float32)
    i = np.arange(128)[:, None]
    mm = np.arange(512)[None, :]
    m[(mm - 384) > i] = NEG_BIG
    return m


def _attn_in_maps(cfg, x, Wq, bq, Wk, bk, Wv, bv, ident_bf, ident_f):
    chi, clo, shi, slo = _rope_tables(cfg.S)
    maps = []
    for c in range(cfg.n_cores):
        r0, r1 = c * cfg.PW, (c + 1) * cfg.PW
        maps.append({
            "x": x,
            "wq_nat": np.ascontiguousarray(Wq[r0:r1, :]),
            "wk_nat": np.ascontiguousarray(Wk[r0:r1, :]),
            "wv_nat": np.ascontiguousarray(Wv[r0:r1, :]),
            "bq_t": np.ascontiguousarray(np.tile(bq[r0:r1], (128, 1))),
            "bk_t": np.ascontiguousarray(np.tile(bk[r0:r1], (128, 1))),
            "bv_t": np.ascontiguousarray(np.tile(bv[r0:r1], (128, 1))),
            "cos_hi": chi, "cos_lo": clo, "sin_hi": shi, "sin_lo": slo,
            "bigmask": _bigmask(),
            "ident_bf": ident_bf, "ident_f": ident_f,
        })
    return maps


def _ffn_in_maps(cfg, x, attT, wg, wu, wd, wo, bo, ident_bf, ident_f):
    bo_t = np.ascontiguousarray(np.tile(bo, (128, 1)))
    maps = []
    for c in range(cfg.n_cores):
        t0, t1 = c * cfg.TOK, (c + 1) * cfg.TOK
        maps.append({
            "x_own": np.ascontiguousarray(x[t0:t1, :]),
            "attT_own": np.ascontiguousarray(attT[:, t0:t1]),
            "wg": wg, "wu": wu, "wd": wd, "wo_nat": wo, "bo_t": bo_t,
            "ident_bf": ident_bf, "ident_f": ident_f,
        })
    return maps


def _fused_in_maps(cfg, x, Wq, bq, Wk, bk, Wv, bv, wg, wu, wd, wo, bo,
                   ident_bf, ident_f):
    maps_a = _attn_in_maps(cfg, x, Wq, bq, Wk, bk, Wv, bv, ident_bf, ident_f)
    bo_t = np.ascontiguousarray(np.tile(bo, (128, 1)))
    maps = []
    for c in range(cfg.n_cores):
        t0, t1 = c * cfg.TOK, (c + 1) * cfg.TOK
        m = dict(maps_a[c])
        m.update({
            "x_own": np.ascontiguousarray(x[t0:t1, :]),
            "wg": wg, "wu": wu, "wd": wd, "wo_nat": wo, "bo_t": bo_t,
            "tok0": np.array([[t0]], dtype=np.int32),
        })
        maps.append(m)
    return maps


def kernel(L0_attn_x, L0_attn_Wq, L0_attn_bq, L0_attn_Wk, L0_attn_bk,
           L0_attn_Wv, L0_attn_bv, L0_attn_Wo, L0_attn_bo,
           L0_ffn_w_gate, L0_ffn_w_up, L0_ffn_w_down):
    import ml_dtypes
    cfg = Cfg()
    B, S, D, NC = cfg.B, cfg.S, cfg.D, cfg.n_cores
    x = np.ascontiguousarray(np.asarray(L0_attn_x, dtype=np.float32).reshape(B * S, D))
    f32 = lambda a: np.ascontiguousarray(np.asarray(a, dtype=np.float32))

    ident_bf = np.eye(128, dtype=ml_dtypes.bfloat16)
    ident_f = np.eye(128, dtype=np.float32)

    nc_f = build_fused(cfg)
    maps = _fused_in_maps(cfg, x, f32(L0_attn_Wq), f32(L0_attn_bq), f32(L0_attn_Wk),
                          f32(L0_attn_bk), f32(L0_attn_Wv), f32(L0_attn_bv),
                          f32(L0_ffn_w_gate), f32(L0_ffn_w_up), f32(L0_ffn_w_down),
                          f32(L0_attn_Wo), f32(L0_attn_bo), ident_bf, ident_f)
    res = run_bass_kernel_spmd(nc_f, maps, core_ids=list(range(NC)))
    y = np.concatenate([res.results[c]["y"] for c in range(NC)], axis=0)
    return y.reshape(B, S, D)


# version 22
# speedup vs baseline: 1.1299x; 1.1299x over previous
import math
import numpy as np
import contextlib
import concourse.bass as bass
import concourse.mybir as mybir
from concourse.bass_utils import run_bass_kernel_spmd
from concourse.alu_op_type import AluOpType as OP

F32 = mybir.dt.float32
BF16 = mybir.dt.bfloat16
AF = mybir.ActivationFunctionType
AX = mybir.AxisListType

NEG_BIG = -1.0e30
STRIPE_CHUNKS = 3


class Cfg:
    def __init__(self, B=2, S=4096, D=2048, H=16, I=8192, n_cores=8):
        self.B, self.S, self.D, self.H, self.I, self.n_cores = B, S, D, H, I, n_cores
        assert D // H == 128 and D % 128 == 0
        assert H % n_cores == 0
        self.HPC = H // n_cores
        self.PW = self.HPC * 128
        self.NDM = D // 128
        self.NT = S // 128
        assert (B * S) % n_cores == 0
        self.TOK = B * S // n_cores
        assert self.TOK % 128 == 0
        self.THALF = min(512, self.TOK)
        assert self.TOK % self.THALF == 0
        self.NHALF = self.TOK // self.THALF
        self.NTT = self.THALF // 128
        self.ICH = 256
        assert I % self.ICH == 0
        self.NIC = I // self.ICH
        self.ECH = min(512, D)
        self.NE = D // self.ECH


ENGINES = ("sync", "tensor", "vector", "scalar", "gpsimd")


class Sem:
    __slots__ = ("h", "v")

    def __init__(self, handle):
        self.h = handle
        self.v = 0


class Rec:

    SELF_SERIAL = ("vector", "scalar", "gpsimd")

    def __init__(self, selfsems):
        self.ops = {e: [] for e in ENGINES}
        self.selfsems = selfsems
        self.last_token = {}

    def emit(self, eng, fn):
        self.ops[eng].append(fn)

    def wait(self, eng, sem, val):
        if val <= 0:
            return
        self.emit(eng, lambda e, s=sem.h, v=val: e.wait_ge(s, v))

    def op(self, eng, method, *args, inc=None, inc_n=1, **kwargs):
        if eng in self.SELF_SERIAL:
            tok = self.last_token.get(eng)
            if tok is not None:
                self.wait(eng, tok[0], tok[1])
            if inc is None:
                inc, inc_n = self.selfsems[eng], 1
        if inc is not None:
            inc.v += inc_n
            if eng in self.SELF_SERIAL:
                self.last_token[eng] = (inc, inc.v)

        def fn(e, m=method, a=args, k=kwargs, s=(None if inc is None else inc.h), n=inc_n):
            inst = getattr(e, m)(*a, **k)
            if s is not None:
                inst.then_inc(s, n)
        self.emit(eng, fn)

    def play(self, nc):
        with nc.Block() as blk:
            for name in ENGINES:
                lst = self.ops[name]
                if not lst:
                    continue

                def section(e, lst=lst):
                    for fn in lst:
                        fn(e)
                getattr(blk, name)(section)


def mk_sems(nc, ctx, names, paired=(), quad=()):
    sm = {}
    for n in names:
        if n in quad:
            sm[n] = [Sem(ctx.enter_context(nc.semaphore(n + chr(97 + i)))) for i in range(4)]
        elif n in paired:
            sm[n] = [Sem(ctx.enter_context(nc.semaphore(n + "a"))),
                     Sem(ctx.enter_context(nc.semaphore(n + "b")))]
        else:
            sm[n] = Sem(ctx.enter_context(nc.semaphore(n)))
    return sm



def tp_group(r, h, sm, blocks, ident, tensor_waits=(), vector_waits=(), evacs=None):
    assert len(blocks) <= 8
    gi = sm["sTP"].v
    r.wait("tensor", sm["sXNT"], gi - 1)
    for s_, v_ in tensor_waits:
        r.wait("tensor", s_, v_)
    bank = h["tp"][gi % 2]
    if evacs is None:
        srcs = [bb[0] for bb in blocks]
        evacs = [(j, 1, bb[1]) for j, bb in enumerate(blocks)]
    else:
        srcs = blocks
    for j, src in enumerate(srcs):
        r.op("tensor", "transpose", inc=(sm["sTP"] if j == len(srcs) - 1 else None),
             out=bank[:, j * 128:(j + 1) * 128], in_=src, identity=ident)
    r.wait("vector", sm["sTP"], sm["sTP"].v)
    for s_, v_ in vector_waits:
        r.wait("vector", s_, v_)
    for j, (s0, ns, dst) in enumerate(evacs):
        r.op("vector", "tensor_copy", inc=(sm["sXNT"] if j == len(evacs) - 1 else None),
             out=dst, in_=bank[:, s0 * 128:(s0 + ns) * 128])



def ln_tile_dma(r, cfg, h, sm, st, ti, x_ap, row0):
    k = st["ln_kd"]
    st["ln_kd"] += 1
    xst = h["xstage"][k % 3]
    if k >= 3:
        r.wait("sync", sm["sLN"], k - 2)
    sx = sm["sX"][k % 4]
    r.op("sync", "dma_start", inc=sx, inc_n=16,
         out=xst[:, :], in_=x_ap[row0 + ti * 128:row0 + ti * 128 + 128, :])


def ln_tile(r, cfg, h, sm, st, ti, x_ap, row0, dsts, evac_waits=()):
    D, NDM = cfg.D, cfg.NDM
    par = ti % 2
    if st["ln_kd"] == st["ln_k"]:
        ln_tile_dma(r, cfg, h, sm, st, ti, x_ap, row0)
    k = st["ln_k"]
    st["ln_k"] += 1
    assert st["ln_kd"] - k in (1, 2, 3)
    xpar = k % 3
    xst = h["xstage"][xpar]
    sx = sm["sX"][k % 4]
    r.wait("vector", sx, sx.v)
    if k >= 2:
        r.wait("vector", sm["sR0"], st["r0_after"][k - 2])
    r.op("vector", "tensor_reduce", out=h["srow"][:, :], in_=xst[:, :],
         axis=AX.X, op=OP.add)
    r.op("vector", "tensor_scalar", inc=sm["sVC"], out=h["mrow"][par][:, :],
         in0=h["srow"][:, :], scalar1=1.0 / D, scalar2=None, op0=OP.mult)
    st["vc_after"].append(sm["sVC"].v)
    r.wait("scalar", sx, sx.v)
    if k >= 2:
        r.wait("scalar", sm["sTP"], st["ln_tp_after"][k - 2])
    r.op("scalar", "activation", inc=sm["sSQ"],
         out=h["xn_lo_tm"][k % 2][:, :], in_=xst[:, :], func=AF.Square,
         accum_out=h["ssrow"][:, :])
    r.wait("vector", sm["sSQ"], sm["sSQ"].v)
    if k >= 2:
        r.wait("vector", sm["sTP"], st["ln_tp_after"][k - 2])
    r.op("vector", "tensor_tensor", out=h["t0"][:, :], in0=h["mrow"][par][:, :],
         in1=h["mrow"][par][:, :], op=OP.mult)
    r.op("vector", "scalar_tensor_tensor", out=h["vpe"][par][:, :], in0=h["ssrow"][:, :],
         scalar=1.0 / D, in1=h["t0"][:, :], op0=OP.mult, op1=OP.subtract)
    r.op("vector", "tensor_scalar", out=h["vpe"][par][:, :], in0=h["vpe"][par][:, :],
         scalar1=1e-5, scalar2=None, op0=OP.add)
    r.op("vector", "reciprocal", inc=sm["sVC"],
         out=h["inv"][par][:, :], in_=h["vpe"][par][:, :])
    r.wait("scalar", sm["sVC"], sm["sVC"].v)
    r.op("scalar", "activation", inc=sm["sR0"],
         out=h["r0"][par][:, :], in_=h["inv"][par][:, :], func=AF.Sqrt)
    st["r0_after"].append(sm["sR0"].v)
    r.wait("vector", sm["sR0"], sm["sR0"].v)
    r.op("vector", "tensor_tensor", out=h["t0"][:, :], in0=h["r0"][par][:, :],
         in1=h["r0"][par][:, :], op=OP.mult)
    r.op("vector", "tensor_tensor", out=h["t0"][:, :], in0=h["t0"][:, :],
         in1=h["vpe"][par][:, :], op=OP.mult)
    r.op("vector", "tensor_scalar", out=h["t0"][:, :], in0=h["t0"][:, :],
         scalar1=-0.5, scalar2=1.5, op0=OP.mult, op1=OP.add)
    r.op("vector", "tensor_tensor", out=h["rrow"][par][:, :],
         in0=h["r0"][par][:, :], in1=h["t0"][:, :], op=OP.mult)
    r.op("vector", "scalar_tensor_tensor", inc=sm["sRR"], out=h["nmr"][par][:, :],
         in0=h["mrow"][par][:, :], scalar=-1.0, in1=h["rrow"][par][:, :],
         op0=OP.mult, op1=OP.mult)
    hi_tm, lo_tm = h["xn_hi_tm"][k % 2], h["xn_lo_tm"][k % 2]
    r.wait("scalar", sm["sRR"], sm["sRR"].v)
    r.op("scalar", "activation", inc=sm["sHI"],
         out=hi_tm[:, :], in_=xst[:, :], func=AF.Identity,
         scale=h["rrow"][par][:, :], bias=h["nmr"][par][:, :])
    st["hi_after"].append(sm["sHI"].v)
    r.wait("vector", sm["sHI"], sm["sHI"].v)
    r.op("vector", "tensor_scalar", out=h["lntmp"][:, :], in0=hi_tm[:, :],
         scalar1=h["nmr"][par][:, :], scalar2=None, op0=OP.subtract)
    r.op("vector", "scalar_tensor_tensor", inc=sm["sLN"],
         out=lo_tm[:, :], in0=xst[:, :], scalar=h["rrow"][par][:, :],
         in1=h["lntmp"][:, :], op0=OP.mult, op1=OP.subtract)
    if dsts is not None:
        items = [("hi", hi_tm, c) for c in range(NDM)] + [("lo", lo_tm, c) for c in range(NDM)]
        first = True
        for g0 in range(0, len(items), 8):
            grp = items[g0:g0 + 8]
            srcs = [t[1][:, t[2] * 128:(t[2] + 1) * 128] for t in grp]
            evacs = []
            j = 0
            while j < len(grp):
                which, _, c0 = grp[j]
                n = 1
                while j + n < len(grp) and grp[j + n][0] == which:
                    n += 1
                evacs.append((j, n, dsts(which, c0, n)))
                j += n
            tw = [(sm["sLN"], sm["sLN"].v)] if first else ()
            tp_group(r, h, sm, srcs, h["ident_bf"][:, :],
                     tensor_waits=tw, vector_waits=(evac_waits if first else ()),
                     evacs=evacs)
            first = False
    st["ln_tp_after"].append(sm["sTP"].v)


def ln_handles(nc, ctx, h, cfg):
    sb = lambda name, shape, dt: ctx.enter_context(nc.sbuf_tensor(name, shape, dt))
    D = cfg.D
    h["xstage"] = [sb(f"xstage{i}", [128, D], F32) for i in range(3)]
    h["lntmp"] = sb("lntmp", [128, D], F32)
    for n in ["srow", "ssrow", "t0"]:
        h[n] = sb(n, [128, 1], F32)
    for n in ["vpe", "inv", "r0", "rrow", "mrow", "nmr"]:
        h[n] = [sb(n + str(i), [128, 1], F32) for i in range(2)]
    h["xn_hi_tm"] = [sb(f"xnhitm{i}", [128, D], BF16) for i in range(2)]
    h["xn_lo_tm"] = [sb(f"xnlotm{i}", [128, D], BF16) for i in range(2)]


def ln_state():
    return {"ln_k": 0, "ln_kd": 0, "vc_after": [], "r0_after": [], "hi_after": [],
            "ln_tp_after": []}


LN_SEMS = ["sX", "sVC", "sSQ", "sR0", "sRR", "sHI", "sLN", "sTP", "sXNT"]



def phase_b(nc, cfg, outer_ctx, h, sm, st, selfsems, attT_load, pre_fn=None):
    D, I, TOK = cfg.D, cfg.I, cfg.TOK
    NDM, NTT, NHALF, THALF = cfg.NDM, cfg.NTT, cfg.NHALF, cfg.THALF
    ICH, NIC, ECH, NE = cfg.ICH, cfg.NIC, cfg.ECH, cfg.NE
    NSUB = ICH // 128
    NEB = ECH // 128
    HDM = max(1, NDM // 2)

    x_in = nc.dram_tensor("x_own", [TOK, D], F32, kind="ExternalInput").ap()
    wg_d = nc.dram_tensor("wg", [D, I], F32, kind="ExternalInput").ap()
    wu_d = nc.dram_tensor("wu", [D, I], F32, kind="ExternalInput").ap()
    wd_d = nc.dram_tensor("wd", [I, D], F32, kind="ExternalInput").ap()
    wo_d = nc.dram_tensor("wo_nat", [D, D], F32, kind="ExternalInput").ap()
    bo_d = nc.dram_tensor("bo_t", [128, D], F32, kind="ExternalInput").ap()
    y_out = nc.dram_tensor("y", [TOK, D], F32, kind="ExternalOutput").ap()

    assert HDM * ICH <= D, "half-load must fit the xstage scratch"
    n_wload = (NDM + HDM - 1) // HDM

    with contextlib.ExitStack() as ctx:
        sb = lambda name, shape, dt: ctx.enter_context(nc.sbuf_tensor(name, shape, dt))
        ps = lambda name, shape, dt: ctx.enter_context(nc.psum_tensor(name, shape, dt))

        h["pacc"] = [ps(f"pacc{i}", [128, 512], F32) for i in range(4)]
        h["pdn"] = [ps(f"pdn{i}", [128, 512], F32) for i in range(2)]
        h["y"] = sb("yacc", [128, NTT, D], F32)

        for half in range(NHALF):
            t_row0 = half * THALF

            with contextlib.ExitStack() as fctx:
                fsb = lambda name, shape, dt: fctx.enter_context(
                    nc.sbuf_tensor(f"{name}_h{half}", shape, dt))
                h["xnt_hi"] = fsb("xnthi", [128, NTT, NDM, 128], BF16)
                h["xnt_lo"] = fsb("xntlo", [128, NTT, NDM, 128], BF16)
                h["wgu_hi"] = [fsb(f"wguhi{i}", [128, NDM, ICH], BF16) for i in range(2)]
                h["wgu_lo"] = [fsb(f"wgulo{i}", [128, NDM, ICH], BF16) for i in range(2)]
                h["wd_hi"] = [fsb(f"wdhi{i}", [128, NSUB, D], BF16) for i in range(2)]
                h["wd_lo"] = [fsb(f"wdlo{i}", [128, NSUB, D], BF16) for i in range(2)]
                h["sgall"] = fsb("sgall", [128, NTT, ICH], F32)
                h["hall"] = fsb("hall", [128, NTT, ICH], F32)
                h["habs"] = fsb("habs", [128, NTT, ICH], F32)
                h["g6"] = [fsb(f"g6_{i}", [128, NTT * (ICH // 4)], BF16) for i in range(6)]
                h["r4"] = [fsb(f"r4_{i}", [128, NTT * (ICH // 4)], BF16) for i in range(4)]
                h["hs_hi"] = fsb("hshi", [128, NTT, ICH], BF16)
                h["hs_lo"] = fsb("hslo", [128, NTT, ICH], BF16)
                h["hT_hi"] = fsb("hthi", [128, NSUB, NTT * 128], BF16)
                h["hT_lo"] = fsb("htlo", [128, NSUB, NTT * 128], BF16)

                r = Rec(selfsems)
                if half == 0 and pre_fn is not None:
                    pre_fn(r)
                for t in range(NTT):
                    r.op("sync", "dma_start", inc=sm["sYI"], inc_n=16,
                         out=h["y"][:, t, :], in_=bo_d[:, :])
                for ti in range(NTT):
                    def dsts(which, c0, n, ti=ti):
                        t_ = h["xnt_hi"] if which == "hi" else h["xnt_lo"]
                        return t_[:, ti, c0:c0 + n, :]
                    ln_tile(r, cfg, h, sm, st, ti, x_in, t_row0, dsts)
                r.play(nc)

                r = Rec(selfsems)
                ffn_pending = None

                wstage_last = {0: 0, 1: 0}

                def wload_split(loads, guard):
                    for sp_, stg, src, dhi, dlo in loads:
                        eng = "sync" if sp_ == 0 else "gpsimd"
                        sx = sm["sWDMA"][sp_]
                        r.wait(eng, sm["sSPL"], wstage_last[sp_])
                        r.op(eng, "dma_start", inc=sx, inc_n=16, out=stg, in_=src)
                        r.wait("vector", sx, sx.v)
                        if guard is not None:
                            r.wait("vector", guard[0], guard[1])
                            guard = None
                        r.op("vector", "tensor_copy", out=dhi, in_=stg)
                        r.op("vector", "scalar_tensor_tensor", inc=sm["sSPL"],
                             out=dlo, in0=stg, scalar=1.0, in1=dhi,
                             op0=OP.mult, op1=OP.subtract)
                        wstage_last[sp_] = sm["sSPL"].v

                def emit_wload(ic):
                    i0 = ic * ICH
                    for wsrc, slot, gsem in ((wg_d, 0, sm["sMMg"]),
                                             (wu_d, 1, sm["sMMu"])):
                        gval = gsem.v
                        loads = []
                        for hb in range(n_wload):
                            c0 = hb * HDM
                            ndm = min(HDM, NDM - c0)
                            src = bass.AP(wsrc.tensor, c0 * 128 * I + i0,
                                          [[I, 128], [128 * I, ndm], [1, ICH]])
                            stg = bass.AP(h["xstage"][hb % 2], 0,
                                          [[cfg.D, 128], [ICH, ndm], [1, ICH]])
                            loads.append((hb % 2, stg, src,
                                          h["wgu_hi"][slot][:, c0:c0 + ndm, :],
                                          h["wgu_lo"][slot][:, c0:c0 + ndm, :]))
                        wload_split(loads, (gsem, gval))
                    loads = []
                    gp_ = ic % 2
                    for hb in range(NSUB):
                        src = bass.AP(wd_d.tensor, (i0 + hb * 128) * D, [[D, 128], [1, D]])
                        stg = bass.AP(h["xstage"][hb % 2], 0, [[D, 128], [1, D]])
                        loads.append((hb % 2, stg, src, h["wd_hi"][gp_][:, hb, :],
                                      h["wd_lo"][gp_][:, hb, :]))
                    wload_split(loads, (sm["sDNt"], sm["sDNt"].v))
                    return sm["sSPL"].v

                spl_next = emit_wload(0)
                for ic in range(NIC):
                    g = half * NIC + ic
                    gp = ic % 2
                    spl_done = spl_next

                    for t in range(NTT):
                        gb = h["pacc"][t % 4]
                        r.wait("tensor", sm["sSPL"], spl_done - 2 - NSUB)
                        r.wait("tensor", sm["sHmul"], sm["sHmul"].v - (NTT - 1 - t))
                        for pi in range(3):
                            xs = ("xnt_hi", "xnt_lo", "xnt_hi")[pi]
                            w = h["wgu_hi"][0] if pi != 2 else h["wgu_lo"][0]
                            for c in range(NDM):
                                r.op("tensor", "matmul", gb[:, 0:ICH],
                                     h[xs][:, t, c, :], w[:, c, :],
                                     start=(pi == 0 and c == 0),
                                     stop=(pi == 2 and c == NDM - 1),
                                     inc=(sm["sMMg"] if pi == 2 and c == NDM - 1 else None))
                        r.wait("scalar", sm["sMMg"], sm["sMMg"].v)
                        r.op("scalar", "activation", inc=sm["sSG"],
                             out=h["sgall"][:, t, :], in_=gb[:, 0:ICH], func=AF.Silu)
                    for t in range(NTT):
                        ub = h["pacc"][t % 4]
                        r.wait("tensor", sm["sSPL"], spl_done - NSUB)
                        r.wait("tensor", sm["sSG"], sm["sSG"].v - (NTT - 1 - t))
                        r.wait("tensor", sm["sHmul"], sm["sHmul"].v - 3)
                        for pi in range(3):
                            xs = ("xnt_hi", "xnt_lo", "xnt_hi")[pi]
                            w = h["wgu_hi"][1] if pi != 2 else h["wgu_lo"][1]
                            for c in range(NDM):
                                r.op("tensor", "matmul", ub[:, 0:ICH],
                                     h[xs][:, t, c, :], w[:, c, :],
                                     start=(pi == 0 and c == 0),
                                     stop=(pi == 2 and c == NDM - 1),
                                     inc=(sm["sMMu"] if pi == 2 and c == NDM - 1 else None))
                        r.wait("vector", sm["sMMu"], sm["sMMu"].v)
                        r.op("vector", "tensor_tensor", inc=sm["sHmul"],
                             out=h["hall"][:, t, :], in0=h["sgall"][:, t, :], in1=ub[:, 0:ICH],
                             op=OP.mult)
                        r.wait("scalar", sm["sHmul"], sm["sHmul"].v)
                        r.wait("scalar", sm["sHspl"], sm["sHspl"].v)
                        r.op("scalar", "activation", inc=sm["sAbs"],
                             out=h["habs"][:, t, :], in_=h["hall"][:, t, :], func=AF.Abs)
                    if ffn_pending is not None:
                        ffn_pending()
                        ffn_pending = None
                    if ic + 1 < NIC:
                        spl_next = emit_wload(ic + 1)
                    r.wait("vector", sm["sAbs"], sm["sAbs"].v)

                    def plane(tns, kk):
                        return tns[:, :, kk::4]
                    g6, r4 = h["g6"], h["r4"]
                    for i, (a_, b_) in enumerate([(1, 0), (2, 0), (3, 0), (2, 1), (3, 1), (3, 2)]):
                        r.op("vector", "tensor_tensor", out=g6[i][:, :], in0=plane(h["habs"], a_),
                             in1=plane(h["habs"], b_), op=OP.is_gt)
                    r.op("vector", "tensor_tensor", out=r4[0][:, :], in0=g6[0][:, :], in1=g6[1][:, :], op=OP.add)
                    r.op("vector", "tensor_tensor", out=r4[0][:, :], in0=r4[0][:, :], in1=g6[2][:, :], op=OP.add)
                    r.op("vector", "tensor_tensor", out=r4[1][:, :], in0=g6[3][:, :], in1=g6[4][:, :], op=OP.add)
                    r.op("vector", "scalar_tensor_tensor", out=r4[1][:, :], in0=g6[0][:, :], scalar=-1.0,
                         in1=r4[1][:, :], op0=OP.mult, op1=OP.add)
                    r.op("vector", "tensor_scalar", out=r4[1][:, :], in0=r4[1][:, :], scalar1=1.0,
                         scalar2=None, op0=OP.add)
                    r.op("vector", "tensor_tensor", out=r4[2][:, :], in0=g6[1][:, :], in1=g6[3][:, :], op=OP.add)
                    r.op("vector", "scalar_tensor_tensor", out=r4[2][:, :], in0=r4[2][:, :], scalar=-1.0,
                         in1=g6[5][:, :], op0=OP.mult, op1=OP.add)
                    r.op("vector", "tensor_scalar", out=r4[2][:, :], in0=r4[2][:, :], scalar1=2.0,
                         scalar2=None, op0=OP.add)
                    r.op("vector", "tensor_tensor", out=r4[3][:, :], in0=g6[2][:, :], in1=g6[4][:, :], op=OP.add)
                    r.op("vector", "tensor_tensor", out=r4[3][:, :], in0=r4[3][:, :], in1=g6[5][:, :], op=OP.add)
                    r.op("vector", "tensor_scalar", out=r4[3][:, :], in0=r4[3][:, :], scalar1=-1.0,
                         scalar2=3.0, op0=OP.mult, op1=OP.add)
                    for kk in range(4):
                        r.op("vector", "tensor_scalar", out=r4[kk][:, :], in0=r4[kk][:, :], scalar1=1.5,
                             scalar2=None, op0=OP.is_lt)
                        r.op("vector", "tensor_tensor",
                             out=plane(h["hall"], kk),
                             in0=plane(h["hall"], kk), in1=r4[kk][:, :], op=OP.mult)
                    r.wait("vector", sm["sTP"], sm["sTP"].v)
                    r.op("vector", "tensor_copy", out=h["hs_hi"][:, :, :], in_=h["hall"][:, :, :])
                    r.op("vector", "scalar_tensor_tensor", inc=sm["sHspl"],
                         out=h["hs_lo"][:, :, :], in0=h["hall"][:, :, :], scalar=1.0,
                         in1=h["hs_hi"][:, :, :], op0=OP.mult, op1=OP.subtract)
                    def make_td(gp=gp, spl_done=spl_done):
                        hspl_v = sm["sHspl"].v
                        dnt_v = sm["sDNt"].v

                        def td():
                            items = []
                            for s_ in range(2):
                                srcs = h["hs_hi"] if s_ == 0 else h["hs_lo"]
                                dstt = h["hT_hi"] if s_ == 0 else h["hT_lo"]
                                for jj in range(NSUB):
                                    for t in range(NTT):
                                        items.append((srcs[:, t, jj * 128:(jj + 1) * 128],
                                                      dstt, jj, t))
                            first = True
                            for g0 in range(0, len(items), 8):
                                grp = items[g0:g0 + 8]
                                srcs_ = [it[0] for it in grp]
                                evacs = []
                                j = 0
                                while j < len(grp):
                                    dstt, jj, t0 = grp[j][1], grp[j][2], grp[j][3]
                                    n = 1
                                    while (j + n < len(grp) and grp[j + n][1] is dstt
                                           and grp[j + n][2] == jj and grp[j + n][3] == t0 + n):
                                        n += 1
                                    evacs.append((j, n, dstt[:, jj, t0 * 128:(t0 + n) * 128]))
                                    j += n
                                tw = [(sm["sHspl"], hspl_v)] if first else ()
                                vw = [(sm["sDNt"], dnt_v)] if first else ()
                                tp_group(r, h, sm, srcs_, h["ident_bf"][:, :],
                                         tensor_waits=tw, vector_waits=vw, evacs=evacs)
                                first = False
                            tpe_done = sm["sXNT"].v
                            r.wait("tensor", sm["sXNT"], tpe_done)
                            r.wait("tensor", sm["sSPL"], spl_done)
                            for t in range(NTT):
                                for e in range(NE):
                                    kde = sm["sDNt"].v
                                    pb = h["pdn"][kde % 2]
                                    r.wait("tensor", sm["sYA"], sm["sYA"].v - 1)
                                    for pi, (hh, ws) in enumerate((("hT_hi", "wd_hi"), ("hT_lo", "wd_hi"),
                                                                   ("hT_hi", "wd_lo"))):
                                        for j in range(NSUB):
                                            r.op("tensor", "matmul", pb[:, 0:ECH],
                                                 h[hh][:, j, t * 128:(t + 1) * 128],
                                                 h[ws][gp][:, j, e * ECH:(e + 1) * ECH],
                                                 start=(pi == 0 and j == 0),
                                                 stop=(pi == 2 and j == NSUB - 1),
                                                 inc=(sm["sDNt"] if pi == 2 and j == NSUB - 1 else None))
                                    r.wait("vector", sm["sDNt"], sm["sDNt"].v)
                                    r.wait("vector", sm["sYI"], sm["sYI"].v)
                                    r.op("vector", "tensor_tensor", inc=sm["sYA"],
                                         out=h["y"][:, t, e * ECH:(e + 1) * ECH],
                                         in0=h["y"][:, t, e * ECH:(e + 1) * ECH],
                                         in1=pb[:, 0:ECH], op=OP.add)
                        return td
                    ffn_pending = make_td()
                ffn_pending()
                ffn_pending = None
                r.play(nc)

            with contextlib.ExitStack() as wctx:
                wsb = lambda name, shape, dt: wctx.enter_context(
                    nc.sbuf_tensor(f"{name}_h{half}", shape, dt))
                h["attT_stage"] = [wsb(f"atts{i}", [128, 4, THALF], F32) for i in range(2)]
                h["attT_hi"] = wsb("atthi", [128, NDM, THALF], BF16)
                h["attT_lo"] = wsb("attlo", [128, NDM, THALF], BF16)
                h["wo_st"] = wsb("wost", [128, NEB, D], F32)
                h["wo_hi"] = wsb("wohi", [128, NDM, ECH], BF16)
                h["wo_lo"] = wsb("wolo", [128, NDM, ECH], BF16)

                r = Rec(selfsems)
                ATG = 4
                ld_eng = getattr(attT_load, "eng", "sync")
                for gi_, c0 in enumerate(range(0, NDM, ATG)):
                    nch = min(ATG, NDM - c0)
                    if gi_ >= 2:
                        r.wait(ld_eng, sm["sATs"], sm["sATs"].v - 1)
                    sat = sm["sATd"][gi_ % 2]
                    attT_load(r, c0, nch, t_row0, h["attT_stage"][gi_ % 2][:, 0:nch, :], sat)
                    r.wait("scalar", sat, sat.v)
                    r.wait("vector", sat, sat.v)
                    stg = h["attT_stage"][gi_ % 2]
                    for cc in range(nch):
                        c = c0 + cc
                        r.op("scalar", "activation", inc=sm["sSCp"],
                             out=h["attT_hi"][:, c, :], in_=stg[:, cc, :], func=AF.Identity)
                        r.wait("vector", sm["sSCp"], sm["sSCp"].v)
                        r.op("vector", "scalar_tensor_tensor",
                             inc=(sm["sATs"] if cc == nch - 1 else None),
                             out=h["attT_lo"][:, c, :], in0=stg[:, cc, :], scalar=1.0,
                             in1=h["attT_hi"][:, c, :], op0=OP.mult, op1=OP.subtract)
                ats_done = sm["sATs"].v
                for e in range(NE):
                    r.wait("sync", sm["sWoT"], sm["sWoT"].v)
                    r.op("sync", "dma_start", inc=sm["sWoD"], inc_n=16, out=h["wo_st"][:, :, :],
                         in_=bass.AP(wo_d.tensor, (e * ECH) * D, [[D, 128], [128 * D, NEB], [1, D]]))
                    r.wait("tensor", sm["sWoD"], sm["sWoD"].v)
                    womm_before = sm["sWoMM"].v
                    for f_ in range(NDM):
                        g0 = f_
                        grp = [(f_, se) for se in range(NEB)]
                        gi = sm["sWoT"].v
                        r.wait("tensor", sm["sWoTe"], gi - 1)
                        bank = h["pdn"][gi % 2]
                        for j, (f, se) in enumerate(grp):
                            r.op("tensor", "transpose", inc=(sm["sWoT"] if j == len(grp) - 1 else None),
                                 out=bank[:, j * 128:(j + 1) * 128],
                                 in_=h["wo_st"][:, se, f * 128:(f + 1) * 128],
                                 identity=h["ident_f"][:, :])
                        r.wait("vector", sm["sWoT"], sm["sWoT"].v)
                        if f_ == 0 and e > 0:
                            r.wait("vector", sm["sWoMM"], womm_before)
                        f0 = grp[0][0]
                        se0 = grp[0][1]
                        nsl = len(grp)
                        assert all(f == f0 for f, _ in grp)
                        dst_hi = h["wo_hi"][:, f0, se0 * 128:(se0 + nsl) * 128]
                        dst_lo = h["wo_lo"][:, f0, se0 * 128:(se0 + nsl) * 128]
                        r.op("vector", "tensor_copy", out=dst_hi, in_=bank[:, 0:nsl * 128])
                        r.op("vector", "scalar_tensor_tensor", inc=sm["sWoTe"],
                             out=dst_lo, in0=bank[:, 0:nsl * 128], scalar=1.0,
                             in1=dst_hi, op0=OP.mult, op1=OP.subtract)
                    wote_done = sm["sWoTe"].v
                    for t in range(NTT):
                        pb = h["pacc"][sm["sWoMM"].v % 4]
                        r.wait("tensor", sm["sWoTe"], wote_done)
                        r.wait("tensor", sm["sATs"], ats_done)
                        r.wait("tensor", sm["sWoYA"], sm["sWoYA"].v - 3)
                        for pi, (at, ws) in enumerate((("attT_hi", "wo_hi"), ("attT_lo", "wo_hi"),
                                                       ("attT_hi", "wo_lo"))):
                            for f in range(NDM):
                                r.op("tensor", "matmul", pb[:, 0:ECH],
                                     h[at][:, f, t * 128:(t + 1) * 128], h[ws][:, f, :],
                                     start=(pi == 0 and f == 0),
                                     stop=(pi == 2 and f == NDM - 1),
                                     inc=(sm["sWoMM"] if pi == 2 and f == NDM - 1 else None))
                        r.wait("vector", sm["sWoMM"], sm["sWoMM"].v)
                        r.op("vector", "tensor_tensor", inc=sm["sWoYA"],
                             out=h["y"][:, t, e * ECH:(e + 1) * ECH],
                             in0=h["y"][:, t, e * ECH:(e + 1) * ECH],
                             in1=pb[:, 0:ECH], op=OP.add)
                for t in range(NTT):
                    row = t_row0 + t * 128
                    if t >= 2:
                        r.wait("sync", sm["sYR"], sm["sYR"].v - 1)
                    sre = sm["sRes"][t % 2]
                    r.op("sync", "dma_start", inc=sre, inc_n=16,
                         out=h["xstage"][t % 2][:, :], in_=x_in[row:row + 128, :])
                    r.wait("vector", sre, sre.v)
                    r.op("vector", "tensor_tensor", inc=sm["sYR"],
                         out=h["y"][:, t, :], in0=h["y"][:, t, :], in1=h["xstage"][t % 2][:, :],
                         op=OP.add)
                    r.wait("sync", sm["sYR"], sm["sYR"].v)
                    r.op("sync", "dma_start", inc=sm["sYO"], inc_n=16,
                         out=y_out[row:row + 128, :], in_=h["y"][:, t, :])
                r.wait("sync", sm["sYO"], sm["sYO"].v)
                r.play(nc)


B_SEMS = ["sCst", "sATd", "sATs", "sWDMA", "sSPL", "sMMg", "sSG", "sMMu", "sHmul",
          "sAbs", "sHspl", "sDNt", "sYA", "sYI", "sSCp", "sMSK",
          "sWoD", "sWoT", "sWoTe", "sWoMM", "sWoYA", "sRes", "sYR", "sYO", "sCC"]


def build_ffn(cfg):
    nc = bass.Bass("TRN2", target_bir_lowering=False)
    attT = nc.dram_tensor("attT_own", [cfg.D, cfg.TOK], F32, kind="ExternalInput").ap()
    identbf_d = nc.dram_tensor("ident_bf", [128, 128], BF16, kind="ExternalInput").ap()
    identf_d = nc.dram_tensor("ident_f", [128, 128], F32, kind="ExternalInput").ap()
    with contextlib.ExitStack() as ctx:
        sb = lambda name, shape, dt: ctx.enter_context(nc.sbuf_tensor(name, shape, dt))
        ps = lambda name, shape, dt: ctx.enter_context(nc.psum_tensor(name, shape, dt))
        h = {"tp": [ps(f"tp{i}", [128, 1024], BF16) for i in range(2)],
             "ident_bf": sb("identbf", [128, 128], BF16),
             "ident_f": sb("identf", [128, 128], F32)}
        ln_handles(nc, ctx, h, cfg)
        sm = mk_sems(nc, ctx, LN_SEMS + B_SEMS, paired={"sATd", "sWDMA", "sRes"}, quad={"sX"})
        selfsems = {e: Sem(ctx.enter_context(nc.semaphore("zz_" + e))) for e in Rec.SELF_SERIAL}
        st = ln_state()
        r = Rec(selfsems)
        r.op("sync", "dma_start", inc=sm["sCst"], inc_n=16, out=h["ident_bf"][:, :], in_=identbf_d[:, :])
        r.op("sync", "dma_start", inc=sm["sCst"], inc_n=16, out=h["ident_f"][:, :], in_=identf_d[:, :])
        r.wait("sync", sm["sCst"], sm["sCst"].v)
        r.play(nc)

        def attT_load(r, c0, nch, t_row0, dst_ap, sem):
            r.op("sync", "dma_start", inc=sem, inc_n=16, out=dst_ap,
                 in_=bass.AP(attT.tensor, (c0 * 128) * cfg.TOK + t_row0,
                             [[cfg.TOK, 128], [128 * cfg.TOK, nch], [1, cfg.THALF]]))
        phase_b(nc, cfg, ctx, h, sm, st, selfsems, attT_load)
    return nc


DBG_ATT = False
DBG_ATTIN = False

def build_fused(cfg):
    nc = bass.Bass("TRN2", target_bir_lowering=False)
    B, S, D, PW = cfg.B, cfg.S, cfg.D, cfg.PW
    dbg_d = nc.dram_tensor("dbg_att", [D, B * S], F32, kind="ExternalOutput").ap() if DBG_ATT else None
    identbf_d = nc.dram_tensor("ident_bf", [128, 128], BF16, kind="ExternalInput").ap()
    identf_d = nc.dram_tensor("ident_f", [128, 128], F32, kind="ExternalInput").ap()
    tok0_d = nc.dram_tensor("tok0", [1, 1], mybir.dt.int32, kind="ExternalInput").ap()
    attT_loc = nc.dram_tensor("attT_loc", [PW, B * S], F32).ap()
    attT_full = nc.dram_tensor("attT_full", [D, B * S], F32).ap()
    with contextlib.ExitStack() as ctx:
        sb = lambda name, shape, dt: ctx.enter_context(nc.sbuf_tensor(name, shape, dt))
        ps = lambda name, shape, dt: ctx.enter_context(nc.psum_tensor(name, shape, dt))
        h = {"tp": [ps(f"tp{i}", [128, 1024], BF16) for i in range(2)],
             "ident_bf": sb("identbf", [128, 128], BF16),
             "ident_f": sb("identf", [128, 128], F32),
             "tok0": sb("tok0sb", [1, 1], mybir.dt.int32)}
        ln_handles(nc, ctx, h, cfg)
        all_sems = LN_SEMS + A_SEMS + [x for x in B_SEMS if x not in A_SEMS]
        sm = mk_sems(nc, ctx, all_sems,
                     paired={"sQTd", "sAO", "sATd", "sWDMA", "sRes"}, quad={"sX"})
        selfsems = {e: Sem(ctx.enter_context(nc.semaphore("zz_" + e))) for e in Rec.SELF_SERIAL}
        st = ln_state()
        r = Rec(selfsems)
        r.op("sync", "dma_start", inc=sm["sCst"], inc_n=16, out=h["ident_bf"][:, :], in_=identbf_d[:, :])
        r.op("sync", "dma_start", inc=sm["sCst"], inc_n=16, out=h["ident_f"][:, :], in_=identf_d[:, :])
        r.op("sync", "dma_start", inc=sm["sCst"], inc_n=16, out=h["tok0"][:, :], in_=tok0_d[:, :])
        r.wait("sync", sm["sCst"], sm["sCst"].v)
        r.play(nc)

        phase_a(nc, cfg, ctx, h, sm, st, selfsems, attT_loc)
        ao_fin = (sm["sAO"][0].v, sm["sAO"][1].v)

        def pre_fn(r):
            r.wait("gpsimd", sm["sAO"][0], ao_fin[0])
            r.wait("gpsimd", sm["sAO"][1], ao_fin[1])
            r.op("gpsimd", "collective_compute", "AllGather", OP.bypass,
                 [[i for i in range(cfg.n_cores)]],
                 [attT_loc.opt()], [attT_full.opt()], inc=sm["sCC"])
            if DBG_ATT:
                r.wait("gpsimd", sm["sCC"], 1)
                r.op("gpsimd", "dma_start", inc=sm["sCC"], inc_n=16,
                     out=dbg_d[:, :], in_=attT_full[:, :])
                r.wait("gpsimd", sm["sCC"], 17)

        regstate = {}
        attT_own_d = (nc.dram_tensor("attT_own", [D, cfg.TOK], F32, kind="ExternalInput").ap()
                      if DBG_ATTIN else None)

        def attT_load(r, c0, nch, t_row0, dst_ap, sem):
            if DBG_ATTIN:
                r.op("sync", "dma_start", inc=sem, inc_n=16, out=dst_ap,
                     in_=bass.AP(attT_own_d.tensor, (c0 * 128) * cfg.TOK + t_row0,
                                 [[cfg.TOK, 128], [128 * cfg.TOK, nch], [1, cfg.THALF]]))
                return
            r.wait("gpsimd", sm["sCC"], 1)

            def fn(e, r=r, c0=c0, nch=nch, t_row0=t_row0, dst_ap=dst_ap, s=sem):
                key = t_row0 // cfg.THALF
                regs = regstate.setdefault(key, {})
                if "r_tok" not in regs:
                    regs["r_tok"] = e.alloc_register(f"r_tok{key}")
                    regs["r_off"] = e.alloc_register(f"r_off{key}")
                    e.reg_load(regs["r_tok"], h["tok0"][0:1, 0:1])
                e.reg_add(regs["r_off"], regs["r_tok"], (c0 * 128) * (B * S) + t_row0)
                e.dma_start(out=dst_ap,
                            in_=bass.AP(attT_full.tensor, regs["r_off"],
                                        [[B * S, 128], [128 * B * S, nch],
                                         [1, cfg.THALF]])).then_inc(s.h, 16)
            sem.v += 16
            r.emit("gpsimd", fn)
        attT_load.eng = "sync" if DBG_ATTIN else "gpsimd"
        phase_b(nc, cfg, ctx, h, sm, st, selfsems, attT_load, pre_fn=pre_fn)
    return nc



def _chunks_of(qt):
    ext = 128 * (qt + 1)
    out, s = [], 0
    while s < ext:
        w = min(512, ext - s)
        out.append((s, w))
        s += w
    return out


def _stripes_of(qt):
    ch = _chunks_of(qt)
    return [ch[i:i + STRIPE_CHUNKS] for i in range(0, len(ch), STRIPE_CHUNKS)]


def phase_a(nc, cfg, outer_ctx, h, sm, st, selfsems, attT_d):
    B, S, D = cfg.B, cfg.S, cfg.D
    NDM, NT, HPC, PW = cfg.NDM, cfg.NT, cfg.HPC, cfg.PW
    x_d = nc.dram_tensor("x", [B * S, D], F32, kind="ExternalInput").ap()
    wq_d = nc.dram_tensor("wq_nat", [PW, D], F32, kind="ExternalInput").ap()
    wk_d = nc.dram_tensor("wk_nat", [PW, D], F32, kind="ExternalInput").ap()
    wv_d = nc.dram_tensor("wv_nat", [PW, D], F32, kind="ExternalInput").ap()
    bq_d = nc.dram_tensor("bq_t", [128, PW], F32, kind="ExternalInput").ap()
    bk_d = nc.dram_tensor("bk_t", [128, PW], F32, kind="ExternalInput").ap()
    bv_d = nc.dram_tensor("bv_t", [128, PW], F32, kind="ExternalInput").ap()
    chi_d = nc.dram_tensor("cos_hi", [S, 64], F32, kind="ExternalInput").ap()
    clo_d = nc.dram_tensor("cos_lo", [S, 64], F32, kind="ExternalInput").ap()
    shi_d = nc.dram_tensor("sin_hi", [S, 64], F32, kind="ExternalInput").ap()
    slo_d = nc.dram_tensor("sin_lo", [S, 64], F32, kind="ExternalInput").ap()
    mask_d = nc.dram_tensor("bigmask", [128, 512], F32, kind="ExternalInput").ap()
    qT_scr = nc.dram_tensor("qT_scr", [B * HPC * 2 * 128, S], BF16).ap()

    sc64 = np.float64(1.0) / np.sqrt(np.float64(128.0))
    sc_hi = float(np.float32(sc64))
    sc_lo = float(np.float32(sc64 - np.float64(np.float32(sc64))))

    with contextlib.ExitStack() as ctx:
        sb = lambda name, shape, dt: ctx.enter_context(nc.sbuf_tensor(name, shape, dt))
        ps = lambda name, shape, dt: ctx.enter_context(nc.psum_tensor(name, shape, dt))
        h["scA"] = ps("scA", [128, 1536], F32)
        h["scB"] = ps("scB", [128, 1536], F32)
        h["mask"] = sb("mask", [128, 512], F32)
        h["cc"] = sb("cc", [128, NT, 64], F32)
        h["ss"] = sb("ss", [128, NT, 64], F32)
        h["bq"] = sb("bq", [1, PW], F32)
        h["bk"] = sb("bk", [1, PW], F32)
        h["bv"] = sb("bv", [1, PW], F32)
        for nm in ("wq", "wk", "wv"):
            h[nm + "_hi"] = sb(nm + "hi", [128, NDM, PW], BF16)
            h[nm + "_lo"] = sb(nm + "lo", [128, NDM, PW], BF16)
        h["xnt_st_hi"] = sb("xntsth", [128, NDM, 128], BF16)
        h["xnt_st_lo"] = sb("xntstl", [128, NDM, 128], BF16)
        h["v_hi"] = sb("vhi", [128, NT, PW], BF16)
        h["v_lo"] = sb("vlo", [128, NT, PW], BF16)
        h["kT_hi"] = sb("kthi", [128, HPC, S], BF16)
        h["kT_lo"] = sb("ktlo", [128, HPC, S], BF16)
        h["prot"] = sb("prot", [128, PW], F32)
        h["ones"] = sb("ones", [1, 128], BF16)
        h["bhl"] = sb("bhl", [1, 6, PW], BF16)
        h["phi"] = [sb(f"phi{i}", [128, PW], BF16) for i in range(2)]
        h["plo"] = [sb(f"plo{i}", [128, PW], BF16) for i in range(2)]
        hw_ = max(128, PW // 2)
        h["rtmp"] = [h["lntmp"][:, 0:hw_], h["lntmp"][:, hw_:2 * hw_]]
        h["qsend"] = sb("qsend", [128, HPC * 2 * 128], BF16)
        h["qt_hi"] = [sb(f"qthi{i}", [128, 128], BF16) for i in range(2)]
        h["qt_lo"] = [sb(f"qtlo{i}", [128, 128], BF16) for i in range(2)]
        h["p_hi"] = sb("p_hi", [128, 1536], BF16)
        h["p_lo"] = sb("p_lo", [128, 1536], BF16)
        h["pT"] = sb("pT", [128, 8, 128], BF16)
        h["negm"] = [sb(f"negm{i}", [128, 1], F32) for i in range(2)]
        h["nmrun"] = [sb(f"nmrun{i}", [128, 1], F32) for i in range(2)]
        h["dm"] = sb("dm", [128, 1], F32)
        h["resc"] = sb("resc", [128, 1], F32)
        h["ls"] = sb("ls", [128, 1], F32)
        h["ncm"] = sb("ncm", [128, 1], F32)
        h["lsc"] = sb("lsc", [128, 4], F32)
        h["lrun"] = sb("lrun", [128, 1], F32)
        h["rl"] = sb("rl", [128, 1], F32)
        h["attacc"] = sb("attacc", [128, 128], F32)
        h["attst"] = sb("attst", [128, 128], F32)

        r = Rec(selfsems)
        r.op("sync", "dma_start", inc=sm["sCst"], inc_n=16, out=h["mask"][:, :], in_=mask_d[:, :])
        for dst, src in (("bq", bq_d), ("bk", bk_d), ("bv", bv_d)):
            r.op("sync", "dma_start", inc=sm["sCst"], inc_n=16, out=h[dst][:, :], in_=src[0:1, :])
        tbl_ap = lambda d: bass.AP(d.tensor, 0, [[64, 128], [128 * 64, NT], [1, 64]])
        if NT * 64 <= cfg.D:
            xs0 = lambda: bass.AP(h["xstage"][0], 0, [[cfg.D, 128], [64, NT], [1, 64]])
        else:
            h["cstage"] = sb("cstage", [128, NT, 64], F32)
            xs0 = lambda: h["cstage"][:, :, :]
        r.op("sync", "dma_start", inc=sm["sCst"], inc_n=16, out=h["cc"][:, :, :], in_=tbl_ap(chi_d))
        r.op("sync", "dma_start", inc=sm["sCst"], inc_n=16, out=xs0(), in_=tbl_ap(clo_d))
        r.wait("vector", sm["sCst"], sm["sCst"].v)
        r.op("vector", "tensor_tensor", inc=sm["sWe"], out=h["cc"][:, :, :], in0=h["cc"][:, :, :],
             in1=xs0(), op=OP.add)
        r.op("sync", "dma_start", inc=sm["sCst2"], inc_n=16, out=h["ss"][:, :, :], in_=tbl_ap(shi_d))
        r.wait("sync", sm["sWe"], 1)
        r.op("sync", "dma_start", inc=sm["sCst2"], inc_n=16, out=xs0(), in_=tbl_ap(slo_d))
        r.wait("vector", sm["sCst2"], sm["sCst2"].v)
        r.op("vector", "tensor_tensor", inc=sm["sWe"], out=h["ss"][:, :, :], in0=h["ss"][:, :, :],
             in1=xs0(), op=OP.add)
        r.op("vector", "tensor_scalar", out=h["lntmp"][0:1, 0:PW], in0=h["bq"][:, :],
             scalar1=sc_lo, scalar2=None, op0=OP.mult)
        r.op("vector", "tensor_scalar", out=h["bq"][:, :], in0=h["bq"][:, :], scalar1=sc_hi,
             scalar2=None, op0=OP.mult)
        r.op("vector", "tensor_tensor", out=h["bq"][:, :], in0=h["bq"][:, :],
             in1=h["lntmp"][0:1, 0:PW], op=OP.add)
        r.op("vector", "memset", h["ones"][0:1, :], 1.0)
        for bi, bn in enumerate(("bq", "bk", "bv")):
            r.op("vector", "tensor_copy", out=h["bhl"][0:1, 2 * bi, :], in_=h[bn][0:1, :])
            r.op("vector", "scalar_tensor_tensor", out=h["bhl"][0:1, 2 * bi + 1, :],
                 in0=h[bn][0:1, :], scalar=1.0, in1=h["bhl"][0:1, 2 * bi, :],
                 op0=OP.mult, op1=OP.subtract)

        we_after = []
        wt_after = []
        slot = 0
        wnat = h["xstage"][1]
        for nm, wdd in (("wq", wq_d), ("wk", wk_d), ("wv", wv_d)):
            for rb in range(PW // 128):
                j = len(wt_after)
                if j >= 1:
                    r.wait("sync", sm["sWt"], wt_after[j - 1])
                r.op("sync", "dma_start", inc=sm["sWn"], inc_n=16,
                     out=wnat[:, :], in_=wdd[rb * 128:(rb + 1) * 128, :])
                r.wait("tensor", sm["sWn"], sm["sWn"].v)
                for c in range(NDM):
                    if slot >= 2:
                        r.wait("tensor", sm["sWe"], we_after[slot - 2])
                    bank = h["scA"] if slot % 2 == 0 else h["scB"]
                    r.op("tensor", "transpose", inc=sm["sWt"],
                         out=bank[:, 0:128], in_=wnat[:, c * 128:(c + 1) * 128],
                         identity=h["ident_f"][:, :])
                    r.wait("vector", sm["sWt"], sm["sWt"].v)
                    dhi = h[nm + "_hi"][:, c, rb * 128:(rb + 1) * 128]
                    dlo = h[nm + "_lo"][:, c, rb * 128:(rb + 1) * 128]
                    if nm == "wq":
                        r.op("vector", "tensor_scalar", out=h["rtmp"][0][:, 0:128], in0=bank[:, 0:128],
                             scalar1=sc_hi, scalar2=None, op0=OP.mult)
                        r.op("vector", "scalar_tensor_tensor", out=h["rtmp"][1][:, 0:128],
                             in0=bank[:, 0:128], scalar=sc_lo, in1=h["rtmp"][0][:, 0:128],
                             op0=OP.mult, op1=OP.add)
                        r.op("vector", "tensor_copy", out=dhi, in_=h["rtmp"][1][:, 0:128])
                        r.op("vector", "scalar_tensor_tensor", inc=sm["sWe"],
                             out=dlo, in0=h["rtmp"][1][:, 0:128], scalar=1.0, in1=dhi,
                             op0=OP.mult, op1=OP.subtract)
                    else:
                        r.op("vector", "tensor_copy", out=dhi, in_=bank[:, 0:128])
                        r.op("vector", "scalar_tensor_tensor", inc=sm["sWe"],
                             out=dlo, in0=bank[:, 0:128], scalar=1.0, in1=dhi,
                             op0=OP.mult, op1=OP.subtract)
                    we_after.append(sm["sWe"].v)
                    slot += 1
                wt_after.append(sm["sWt"].v)
        r.play(nc)

        for b in range(B):
            r = Rec(selfsems)
            prj_after = []
            kt_after = []
            epi_pending = None

            def emit_ln(ti):
                def dsts(which, c0, n):
                    t_ = h["xnt_st_hi"] if which == "hi" else h["xnt_st_lo"]
                    return t_[:, c0:c0 + n, :]
                ev = []
                if ti >= 1:
                    ev = [(sm["sPRJ"], prj_after[ti - 1])]
                ln_tile(r, cfg, h, sm, st, ti, x_d, b * S, dsts, evac_waits=ev)
                return sm["sXNT"].v

            xnt_done_next = emit_ln(0)
            if NT > 1:
                ln_tile_dma(r, cfg, h, sm, st, 1, x_d, b * S)
            for ti in range(NT):
                par = ti % 2
                xnt_done = xnt_done_next
                bank = h["scA"] if ti % 2 == 0 else h["scB"]
                r.wait("tensor", sm["sXNT"], xnt_done)
                r.wait("tensor", sm["sEPI"], sm["sEPI"].v)
                for wi, nm in enumerate(("wq", "wk", "wv")):
                    out = bank[:, wi * 512:wi * 512 + PW]
                    for pi, xs in enumerate(("hi", "lo", "hi")):
                        xn = h["xnt_st_hi"] if xs == "hi" else h["xnt_st_lo"]
                        w = h[nm + "_hi"] if pi != 2 else h[nm + "_lo"]
                        for c in range(NDM):
                            r.op("tensor", "matmul", out,
                                 xn[:, c, :], w[:, c, :],
                                 start=(pi == 0 and c == 0), stop=False)
                    r.op("tensor", "matmul", out, h["ones"][0:1, :], h["bhl"][0:1, 2 * wi, :],
                         start=False, stop=False)
                    r.op("tensor", "matmul", out, h["ones"][0:1, :], h["bhl"][0:1, 2 * wi + 1, :],
                         start=False, stop=True,
                         inc=(sm["sPRJ"] if wi == 2 else None))
                prj_after.append(sm["sPRJ"].v)

                if ti + 2 < NT:
                    ln_tile_dma(r, cfg, h, sm, st, ti + 2, x_d, b * S)
                if ti + 1 < NT:
                    xnt_done_next = emit_ln(ti + 1)
                if epi_pending is not None:
                    epi_pending()

                def make_epi(ti=ti, par=par, bank=bank, prjv=sm["sPRJ"].v):
                    def epi():
                        k_epi = len(kt_after)
                        r.wait("vector", sm["sPRJ"], prjv)
                        if k_epi >= 2:
                            r.wait("vector", sm["sTP"], kt_after[k_epi - 2])
                        rot = h["prot"]
                        for nm in ("q", "k", "v"):
                            wi = {"q": 0, "k": 1, "v": 2}[nm]
                            src_ = bank[:, wi * 512:wi * 512 + PW]
                            if nm in ("q", "k"):
                                e_pl = bank[:, wi * 512:wi * 512 + PW:2]
                                o_pl = bank[:, wi * 512 + 1:wi * 512 + PW:2]
                                cc = bass.AP(h["cc"], ti * 64, [[NT * 64, 128], [0, HPC], [1, 64]])
                                ss_ = bass.AP(h["ss"], ti * 64, [[NT * 64, 128], [0, HPC], [1, 64]])
                                re, ro = rot[:, 0:PW:2], rot[:, 1:PW:2]
                                t0_, t1_ = h["rtmp"][0][:, 0:PW // 2], h["rtmp"][1][:, 0:PW // 2]
                                r.op("vector", "tensor_tensor", out=t0_, in0=e_pl, in1=cc, op=OP.mult)
                                r.op("vector", "tensor_tensor", out=t1_, in0=o_pl, in1=ss_, op=OP.mult)
                                r.op("vector", "tensor_tensor", out=re, in0=t0_, in1=t1_, op=OP.subtract)
                                r.op("vector", "tensor_tensor", out=t0_, in0=e_pl, in1=ss_, op=OP.mult)
                                r.op("vector", "tensor_tensor", out=t1_, in0=o_pl, in1=cc, op=OP.mult)
                                r.op("vector", "tensor_tensor", out=ro, in0=t0_, in1=t1_, op=OP.add)
                                src_f = rot[:, :]
                            else:
                                src_f = src_
                            if nm == "v":
                                dhi, dlo = h["v_hi"][:, ti, :], h["v_lo"][:, ti, :]
                                r.op("vector", "tensor_copy", out=dhi, in_=src_f)
                                r.op("vector", "scalar_tensor_tensor", inc=sm["sEPI"],
                                     out=dlo, in0=src_f, scalar=1.0, in1=dhi,
                                     op0=OP.mult, op1=OP.subtract)
                                continue
                            dhi, dlo = h["phi"][par][:, :], h["plo"][par][:, :]
                            r.op("vector", "tensor_copy", out=dhi, in_=src_f)
                            r.op("vector", "scalar_tensor_tensor", inc=sm["sKSP"],
                                 out=dlo, in0=src_f, scalar=1.0, in1=dhi,
                                 op0=OP.mult, op1=OP.subtract)
                            if nm == "q":
                                srcs = []
                                for hd in range(HPC):
                                    srcs.append(h["phi"][par][:, hd * 128:(hd + 1) * 128])
                                    srcs.append(h["plo"][par][:, hd * 128:(hd + 1) * 128])
                                sqs = sm["sQSd"]
                                assert len(srcs) <= 8
                                tp_group(r, h, sm, srcs, h["ident_bf"][:, :],
                                         tensor_waits=[(sm["sKSP"], sm["sKSP"].v)],
                                         vector_waits=[(sqs, sqs.v)],
                                         evacs=[(0, len(srcs), h["qsend"][:, 0:len(srcs) * 128])])
                                r.wait("sync", sm["sXNT"], sm["sXNT"].v)
                                dst = bass.AP(qT_scr.tensor,
                                              (b * HPC * 2 * 128) * S + ti * 128,
                                              [[S, 128], [128 * S, HPC * 2], [1, 128]])
                                r.op("sync", "dma_start", inc=sqs, inc_n=16,
                                     out=dst, in_=h["qsend"][:, :])
                            else:
                                srcs = ([h["phi"][par][:, hd * 128:(hd + 1) * 128] for hd in range(HPC)]
                                        + [h["plo"][par][:, hd * 128:(hd + 1) * 128] for hd in range(HPC)])
                                assert len(srcs) <= 8
                                evacs = [(0, HPC, h["kT_hi"][:, :, ti * 128:(ti + 1) * 128]),
                                         (HPC, HPC, h["kT_lo"][:, :, ti * 128:(ti + 1) * 128])]
                                tp_group(r, h, sm, srcs, h["ident_bf"][:, :],
                                         tensor_waits=[(sm["sKSP"], sm["sKSP"].v)], evacs=evacs)
                        kt_after.append(sm["sTP"].v)
                    return epi
                epi_pending = make_epi()
            epi_pending()
            epi_pending = None
            r.wait("sync", sm["sQSd"], sm["sQSd"].v)
            r.play(nc)

            r = Rec(selfsems)
            probs_tp_guard = 0
            probs_lo_guard = 0
            fin_pending = None
            for u in range(HPC):
                for qt in range(NT):
                    stripes = _stripes_of(qt)
                    nstr = len(stripes)
                    qp = qt % 2
                    sq = sm["sQTd"][qp]
                    r.wait("sync", sm["sSC"], sm["sSC"].v - 1)
                    base = (b * HPC * 2 + u * 2) * 128
                    r.op("sync", "dma_start", inc=sq, inc_n=16,
                         out=h["qt_hi"][qp][:, :],
                         in_=bass.AP(qT_scr.tensor, base * S + qt * 128, [[S, 128], [1, 128]]))
                    r.op("sync", "dma_start", inc=sq, inc_n=16,
                         out=h["qt_lo"][qp][:, :],
                         in_=bass.AP(qT_scr.tensor, (base + 128) * S + qt * 128, [[S, 128], [1, 128]]))
                    sc_after = {}

                    def emit_qk(si):
                        chunks = stripes[si]
                        bank = h["scA"] if si % 2 == 0 else h["scB"]
                        r.wait("tensor", sq, sq.v)
                        r.wait("tensor", sm["sFOLD"], sm["sFOLD"].v)
                        r.wait("tensor", sm["sATe"], sm["sATe"].v)
                        for cj, (k0, w) in enumerate(chunks):
                            for pi, (qb, kb) in enumerate((("qt_hi", "kT_hi"), ("qt_hi", "kT_lo"),
                                                           ("qt_lo", "kT_hi"))):
                                r.op("tensor", "matmul", bank[:, cj * 512:cj * 512 + w],
                                     h[qb][qp][:, :], h[kb][:, u, k0:k0 + w],
                                     start=(pi == 0), stop=(pi == 2),
                                     inc=(sm["sSC"] if pi == 2 else None))
                        sc_after[si] = sm["sSC"].v - len(chunks)

                    emit_qk(0)
                    if fin_pending is not None:
                        fin_pending()
                        fin_pending = None
                    for si, chunks in enumerate(stripes):
                        bank = h["scA"] if si % 2 == 0 else h["scB"]
                        str_w = sum(w for (_, w) in chunks)
                        nsub = str_w // 128
                        if si + 1 < nstr:
                            emit_qk(si + 1)
                        sp = si % 2
                        nch = len(chunks)
                        for cj, (k0, w) in enumerate(chunks):
                            r.wait("vector", sm["sSC"], sc_after[si] + cj + 1)
                            if si == nstr - 1 and cj == nch - 1:
                                r.op("vector", "tensor_tensor",
                                     out=bank[:, cj * 512:cj * 512 + w],
                                     in0=bank[:, cj * 512:cj * 512 + w],
                                     in1=h["mask"][:, 512 - w:512], op=OP.add)
                            tgt = h["negm"][sp] if cj == 0 else h["ncm"]
                            r.op("vector", "tensor_reduce", out=tgt[:, :],
                                 in_=bank[:, cj * 512:cj * 512 + w], axis=AX.X,
                                 op=OP.max, negate=True)
                            if cj > 0:
                                r.op("vector", "tensor_tensor", out=h["negm"][sp][:, :],
                                     in0=h["negm"][sp][:, :], in1=h["ncm"][:, :], op=OP.min)
                        if si == 0:
                            r.op("vector", "tensor_copy", inc=sm["sNM"],
                                 out=h["nmrun"][0][:, :], in_=h["negm"][sp][:, :])
                        else:
                            prev, cur = (si - 1) % 2, si % 2
                            r.op("vector", "tensor_tensor", out=h["nmrun"][cur][:, :],
                                 in0=h["nmrun"][prev][:, :], in1=h["negm"][sp][:, :], op=OP.min)
                            r.op("vector", "tensor_tensor", inc=sm["sNM"],
                                 out=h["dm"][:, :], in0=h["nmrun"][cur][:, :],
                                 in1=h["nmrun"][prev][:, :], op=OP.subtract)
                        nm_cur = h["nmrun"][si % 2]
                        r.wait("scalar", sm["sNM"], sm["sNM"].v)
                        if si > 0:
                            r.op("scalar", "activation", out=h["resc"][:, :], in_=h["dm"][:, :],
                                 func=AF.Exp)
                        r.wait("scalar", sm["sLO"], probs_lo_guard)
                        r.wait("scalar", sm["sTP"], probs_tp_guard)
                        exp_base = sm["sEXP"].v
                        for cj, (k0, w) in enumerate(chunks):
                            acc = h["ls"][:, :] if nch == 1 else h["lsc"][:, cj:cj + 1]
                            r.op("scalar", "activation", out=bank[:, cj * 512:cj * 512 + w],
                                 in_=bank[:, cj * 512:cj * 512 + w], func=AF.Exp,
                                 bias=nm_cur[:, :], scale=1.0, accum_out=acc)
                            r.op("scalar", "activation", inc=sm["sEXP"],
                                 out=h["p_hi"][:, cj * 512:cj * 512 + w],
                                 in_=bank[:, cj * 512:cj * 512 + w], func=AF.Copy)
                        att_slot = bank[:, 0:128]
                        pv_before = sm["sPV"].v
                        def hi_tp(cj, w):
                            csub = w // 128
                            r.wait("vector", sm["sEXP"], exp_base + cj + 1)
                            r.op("vector", "scalar_tensor_tensor",
                                 out=h["p_lo"][:, cj * 512:cj * 512 + w],
                                 in0=bank[:, cj * 512:cj * 512 + w], scalar=1.0,
                                 in1=h["p_hi"][:, cj * 512:cj * 512 + w],
                                 op0=OP.mult, op1=OP.subtract)
                            srcs = [h["p_hi"][:, (cj * 4 + j2) * 128:(cj * 4 + j2 + 1) * 128]
                                    for j2 in range(csub)]
                            tw = [(sm["sEXP"], exp_base + cj + 1)]
                            vw = [(sm["sPV"], pv_before)] if cj == 0 else ()
                            s0 = (cj % 2) * 4
                            tp_group(r, h, sm, srcs, h["ident_bf"][:, :],
                                     tensor_waits=tw, vector_waits=vw,
                                     evacs=[(0, csub, h["pT"][:, s0:s0 + csub, :])])
                            return (cj, s0, csub, sm["sXNT"].v)

                        def hi_pv(cj, s0, csub, xnt_v):
                            r.wait("tensor", sm["sXNT"], xnt_v)
                            for pi, vv in enumerate(("v_hi", "v_lo")):
                                for j2 in range(csub):
                                    vt = (si * STRIPE_CHUNKS * 4) + cj * 4 + j2
                                    r.op("tensor", "matmul", att_slot,
                                         h["pT"][:, s0 + j2, :],
                                         h[vv][:, vt, u * 128:(u + 1) * 128],
                                         start=(cj == 0 and pi == 0 and j2 == 0),
                                         stop=False)

                        pv_pend = None
                        for cj, (k0, w) in enumerate(chunks):
                            info = hi_tp(cj, w)
                            if pv_pend is not None:
                                hi_pv(*pv_pend)
                            pv_pend = info
                        hi_pv(*pv_pend)
                        r.wait("vector", sm["sEXP"], exp_base + nch)
                        if nch > 1:
                            r.op("vector", "tensor_tensor", out=h["ls"][:, :],
                                 in0=h["lsc"][:, 0:1], in1=h["lsc"][:, 1:2], op=OP.add)
                            for cj in range(2, nch):
                                r.op("vector", "tensor_tensor", out=h["ls"][:, :],
                                     in0=h["ls"][:, :], in1=h["lsc"][:, cj:cj + 1], op=OP.add)
                        if si == 0:
                            r.op("vector", "tensor_copy", inc=sm["sLO"],
                                 out=h["lrun"][:, :], in_=h["ls"][:, :])
                        else:
                            r.op("vector", "scalar_tensor_tensor", inc=sm["sLO"],
                                 out=h["lrun"][:, :], in0=h["lrun"][:, :],
                                 scalar=h["resc"][:, :], in1=h["ls"][:, :],
                                 op0=OP.mult, op1=OP.add)
                        probs_lo_guard = sm["sLO"].v
                        def lo_tp(gi, r0_):
                            nblk = min(4, nsub - r0_)
                            srcs = [h["p_lo"][:, jj * 128:(jj + 1) * 128]
                                    for jj in range(r0_, r0_ + nblk)]
                            s0 = (gi % 2) * 4
                            tp_group(r, h, sm, srcs, h["ident_bf"][:, :],
                                     tensor_waits=[(sm["sLO"], probs_lo_guard)],
                                     evacs=[(0, nblk, h["pT"][:, s0:s0 + nblk, :])])
                            return (r0_, s0, nblk, sm["sXNT"].v)

                        def lo_pv(r0_, s0, nblk, xnt_v):
                            r.wait("tensor", sm["sXNT"], xnt_v)
                            for j2 in range(nblk):
                                jj = r0_ + j2
                                vt = (si * STRIPE_CHUNKS * 4) + jj
                                last = (jj == nsub - 1)
                                r.op("tensor", "matmul", att_slot,
                                     h["pT"][:, s0 + j2, :],
                                     h["v_hi"][:, vt, u * 128:(u + 1) * 128],
                                     start=False, stop=last,
                                     inc=(sm["sPV"] if last else None))

                        pv_pend = None
                        for gi, r0_ in enumerate(range(0, nsub, 4)):
                            info = lo_tp(gi, r0_)
                            if pv_pend is not None:
                                lo_pv(*pv_pend)
                            pv_pend = info
                        lo_pv(*pv_pend)
                        probs_tp_guard = sm["sTP"].v
                        r.wait("vector", sm["sPV"], sm["sPV"].v)
                        if si == 0:
                            r.op("vector", "tensor_copy", inc=sm["sFOLD"],
                                 out=h["attacc"][:, :], in_=att_slot)
                        else:
                            r.op("vector", "scalar_tensor_tensor", inc=sm["sFOLD"],
                                 out=h["attacc"][:, :], in0=h["attacc"][:, :],
                                 scalar=h["resc"][:, :], in1=att_slot, op0=OP.mult, op1=OP.add)
                    sa = sm["sAO"][qp]
                    r.wait("vector", sm["sAO"][0], sm["sAO"][0].v)
                    r.wait("vector", sm["sAO"][1], sm["sAO"][1].v)
                    r.op("vector", "reciprocal", out=h["rl"][:, :], in_=h["lrun"][:, :])
                    r.op("vector", "tensor_scalar", inc=sm["sANRM"], out=h["attst"][:, :],
                         in0=h["attacc"][:, :], scalar1=h["rl"][:, :], scalar2=None, op0=OP.mult)

                    def make_fin(u=u, qt=qt, qp=qp, sa=sa, anrm=sm["sANRM"].v):
                        def fin():
                            r.wait("tensor", sm["sANRM"], anrm)
                            r.op("tensor", "transpose", inc=sm["sATT"],
                                 out=h["scB"][:, 128:256], in_=h["attst"][:, :],
                                 identity=h["ident_f"][:, :])
                            r.wait("vector", sm["sATT"], sm["sATT"].v)
                            r.op("vector", "tensor_copy", inc=sm["sATe"],
                                 out=h["attst"][:, :], in_=h["scB"][:, 128:256])
                            r.wait("sync", sm["sATe"], sm["sATe"].v)
                            r.op("sync", "dma_start", inc=sa, inc_n=16,
                                 out=attT_d[u * 128:(u + 1) * 128,
                                            b * S + qt * 128:b * S + qt * 128 + 128],
                                 in_=h["attst"][:, :])
                        return fin
                    fin_pending = make_fin()
            fin_pending()
            fin_pending = None
            r.wait("sync", sm["sAO"][0], sm["sAO"][0].v)
            r.wait("sync", sm["sAO"][1], sm["sAO"][1].v)
            r.play(nc)


A_SEMS = ["sCst", "sCst2", "sWn", "sWt", "sWe", "sPRJ", "sEPI", "sKSP",
          "sQSd", "sQTd", "sSC", "sNM", "sEXP", "sLO", "sPV", "sFOLD",
          "sANRM", "sATT", "sATe", "sAO"]


def build_attn(cfg):
    nc = bass.Bass("TRN2", target_bir_lowering=False)
    attT_d = nc.dram_tensor("attT", [cfg.PW, cfg.B * cfg.S], F32, kind="ExternalOutput").ap()
    identbf_d = nc.dram_tensor("ident_bf", [128, 128], BF16, kind="ExternalInput").ap()
    identf_d = nc.dram_tensor("ident_f", [128, 128], F32, kind="ExternalInput").ap()
    with contextlib.ExitStack() as ctx:
        sb = lambda name, shape, dt: ctx.enter_context(nc.sbuf_tensor(name, shape, dt))
        ps = lambda name, shape, dt: ctx.enter_context(nc.psum_tensor(name, shape, dt))
        h = {"tp": [ps(f"tp{i}", [128, 1024], BF16) for i in range(2)],
             "ident_bf": sb("identbf", [128, 128], BF16),
             "ident_f": sb("identf", [128, 128], F32)}
        ln_handles(nc, ctx, h, cfg)
        sm = mk_sems(nc, ctx, LN_SEMS + A_SEMS, paired={"sQTd", "sAO"}, quad={"sX"})
        selfsems = {e: Sem(ctx.enter_context(nc.semaphore("zz_" + e))) for e in Rec.SELF_SERIAL}
        st = ln_state()
        r = Rec(selfsems)
        r.op("sync", "dma_start", inc=sm["sCst"], inc_n=16, out=h["ident_bf"][:, :], in_=identbf_d[:, :])
        r.op("sync", "dma_start", inc=sm["sCst"], inc_n=16, out=h["ident_f"][:, :], in_=identf_d[:, :])
        r.wait("sync", sm["sCst"], sm["sCst"].v)
        r.play(nc)
        phase_a(nc, cfg, ctx, h, sm, st, selfsems, attT_d)
    return nc



def _rope_tables(S):
    half = 64
    inv = 1.0 / 10000.0 ** (np.arange(half, dtype=np.float64) / half)
    ang = np.arange(S, dtype=np.float64)[:, None] * inv[None, :]
    c, s = np.cos(ang), np.sin(ang)
    chi = c.astype(np.float32)
    clo = (c - chi.astype(np.float64)).astype(np.float32)
    shi = s.astype(np.float32)
    slo = (s - shi.astype(np.float64)).astype(np.float32)
    return chi, clo, shi, slo


def _bigmask():
    m = np.zeros((128, 512), dtype=np.float32)
    i = np.arange(128)[:, None]
    mm = np.arange(512)[None, :]
    m[(mm - 384) > i] = NEG_BIG
    return m


def _attn_in_maps(cfg, x, Wq, bq, Wk, bk, Wv, bv, ident_bf, ident_f):
    chi, clo, shi, slo = _rope_tables(cfg.S)
    maps = []
    for c in range(cfg.n_cores):
        r0, r1 = c * cfg.PW, (c + 1) * cfg.PW
        maps.append({
            "x": x,
            "wq_nat": np.ascontiguousarray(Wq[r0:r1, :]),
            "wk_nat": np.ascontiguousarray(Wk[r0:r1, :]),
            "wv_nat": np.ascontiguousarray(Wv[r0:r1, :]),
            "bq_t": np.ascontiguousarray(np.tile(bq[r0:r1], (128, 1))),
            "bk_t": np.ascontiguousarray(np.tile(bk[r0:r1], (128, 1))),
            "bv_t": np.ascontiguousarray(np.tile(bv[r0:r1], (128, 1))),
            "cos_hi": chi, "cos_lo": clo, "sin_hi": shi, "sin_lo": slo,
            "bigmask": _bigmask(),
            "ident_bf": ident_bf, "ident_f": ident_f,
        })
    return maps


def _ffn_in_maps(cfg, x, attT, wg, wu, wd, wo, bo, ident_bf, ident_f):
    bo_t = np.ascontiguousarray(np.tile(bo, (128, 1)))
    maps = []
    for c in range(cfg.n_cores):
        t0, t1 = c * cfg.TOK, (c + 1) * cfg.TOK
        maps.append({
            "x_own": np.ascontiguousarray(x[t0:t1, :]),
            "attT_own": np.ascontiguousarray(attT[:, t0:t1]),
            "wg": wg, "wu": wu, "wd": wd, "wo_nat": wo, "bo_t": bo_t,
            "ident_bf": ident_bf, "ident_f": ident_f,
        })
    return maps


def _fused_in_maps(cfg, x, Wq, bq, Wk, bk, Wv, bv, wg, wu, wd, wo, bo,
                   ident_bf, ident_f):
    maps_a = _attn_in_maps(cfg, x, Wq, bq, Wk, bk, Wv, bv, ident_bf, ident_f)
    bo_t = np.ascontiguousarray(np.tile(bo, (128, 1)))
    maps = []
    for c in range(cfg.n_cores):
        t0, t1 = c * cfg.TOK, (c + 1) * cfg.TOK
        m = dict(maps_a[c])
        m.update({
            "x_own": np.ascontiguousarray(x[t0:t1, :]),
            "wg": wg, "wu": wu, "wd": wd, "wo_nat": wo, "bo_t": bo_t,
            "tok0": np.array([[t0]], dtype=np.int32),
        })
        maps.append(m)
    return maps


def kernel(L0_attn_x, L0_attn_Wq, L0_attn_bq, L0_attn_Wk, L0_attn_bk,
           L0_attn_Wv, L0_attn_bv, L0_attn_Wo, L0_attn_bo,
           L0_ffn_w_gate, L0_ffn_w_up, L0_ffn_w_down):
    import ml_dtypes
    cfg = Cfg()
    B, S, D, NC = cfg.B, cfg.S, cfg.D, cfg.n_cores
    x = np.ascontiguousarray(np.asarray(L0_attn_x, dtype=np.float32).reshape(B * S, D))
    f32 = lambda a: np.ascontiguousarray(np.asarray(a, dtype=np.float32))

    ident_bf = np.eye(128, dtype=ml_dtypes.bfloat16)
    ident_f = np.eye(128, dtype=np.float32)

    nc_f = build_fused(cfg)
    maps = _fused_in_maps(cfg, x, f32(L0_attn_Wq), f32(L0_attn_bq), f32(L0_attn_Wk),
                          f32(L0_attn_bk), f32(L0_attn_Wv), f32(L0_attn_bv),
                          f32(L0_ffn_w_gate), f32(L0_ffn_w_up), f32(L0_ffn_w_down),
                          f32(L0_attn_Wo), f32(L0_attn_bo), ident_bf, ident_f)
    res = run_bass_kernel_spmd(nc_f, maps, core_ids=list(range(NC)))
    y = np.concatenate([res.results[c]["y"] for c in range(NC)], axis=0)
    return y.reshape(B, S, D)
